# version 10 (final)
# baseline (speedup 1.0000x reference)
import numpy as np
import concourse.bass as bass
import concourse.mybir as mybir
from concourse.bass_utils import run_bass_kernel_spmd

F32 = mybir.dt.float32
BF16 = mybir.dt.bfloat16
U32 = mybir.dt.uint32
ACT = mybir.ActivationFunctionType
ALU = mybir.AluOpType

D = 4096
NCH = 32
T1 = 1024
NTB = 2
LMAP = [0, 2, 1, 3]
Q_START, Q_END_EXCL = 611, 4095
RCOUNT = float(Q_END_EXCL - Q_START)
EPS_LN = 1e-5
DBG = False


def ts(i, w):
    return slice(i * w, (i + 1) * w)


class Plan:
    ENGS = ("sync", "gpsimd", "vector", "scalar", "tensor")

    def __init__(self):
        self.q = {e: [] for e in self.ENGS}
        self.cnt = {}

    def bump(self, s, k):
        self.cnt[s] = self.cnt.get(s, 0) + k
        return self.cnt[s]

    def val(self, s):
        return self.cnt.get(s, 0)

    def emit(self, eng, fn):
        self.q[eng].append(fn)


def build_nc():
    nc = bass.Bass("TRN2", target_bir_lowering=False, detect_race_conditions=False)

    X = nc.dram_tensor("X", [2, 128, NCH, T1], F32, kind="ExternalInput")
    WK = nc.dram_tensor("WK", [2, 128, NCH, 512], F32, kind="ExternalInput")
    WV = nc.dram_tensor("WV", [2, 128, NCH, 512], F32, kind="ExternalInput")
    WQ = nc.dram_tensor("WQ", [16, 128, NCH, 256], F32, kind="ExternalInput")
    WO = nc.dram_tensor("WO", [16, 128, NCH, 256], F32, kind="ExternalInput")
    WFD = nc.dram_tensor("WFD", [16, 128, NCH, 256], F32, kind="ExternalInput")
    ACBD = nc.dram_tensor("ACBD", [128, NCH * 128], F32, kind="ExternalInput")
    BCBD = nc.dram_tensor("BCBD", [128, 4096], F32, kind="ExternalInput")
    RWTD = nc.dram_tensor("RWTD", [128, NCH * 8], F32, kind="ExternalInput")
    BQD = nc.dram_tensor("BQD", [128, 32], F32, kind="ExternalInput")
    BOD = nc.dram_tensor("BOD", [128, 32], F32, kind="ExternalInput")
    BFD = nc.dram_tensor("BFD", [128, 32], F32, kind="ExternalInput")
    BKR = nc.dram_tensor("BKR", [1, 1024], F32, kind="ExternalInput")
    BVR = nc.dram_tensor("BVR", [1, 1024], F32, kind="ExternalInput")
    RBR = nc.dram_tensor("RBR", [1, 8], F32, kind="ExternalInput")
    MSK = nc.dram_tensor("MSK", [2, 1, T1], F32, kind="ExternalInput")
    EMKD = nc.dram_tensor("EMKD", [128, 8], F32, kind="ExternalInput")
    SLT = nc.dram_tensor("SLT", [1, 8], U32, kind="ExternalInput")
    OUT = nc.dram_tensor("OUT", [NCH, 128, T1], F32, kind="ExternalOutput")

    XNSP = nc.dram_tensor("XNSP", [2, 128, NCH * T1], BF16, kind="Internal")
    RWS = nc.dram_tensor("RWS", [8, 1024], F32, kind="Internal")
    SH1D = nc.dram_tensor("SH1D", [2, 128, 1032], F32, kind="Internal", addr_space="Shared")
    SH1F = nc.dram_tensor("SH1F", [2, 1, 4], U32, kind="Internal", addr_space="Shared")
    SH2D = nc.dram_tensor("SH2D", [2, 128, 32], F32, kind="Internal", addr_space="Shared")
    SH2F = nc.dram_tensor("SH2F", [2, 1, 4], U32, kind="Internal", addr_space="Shared")

    BUF1 = nc.alloc_sbuf_tensor("BUF1", [128, NCH * T1], BF16)
    BUF2 = nc.alloc_sbuf_tensor("BUF2", [128, NCH * T1], BF16)
    PK = BUF2[:, 0:8192]
    PV = BUF2[:, 8192:16384]
    WR = nc.alloc_sbuf_tensor("WR", [128, 16384], BF16)
    TRI = nc.alloc_sbuf_tensor("TRI", [128, 4096], BF16)
    BCBS = nc.alloc_sbuf_tensor("BCBS", [128, 4096], BF16)
    EXG = nc.alloc_sbuf_tensor("EXG", [128, 3072], F32)
    KVBF = nc.alloc_sbuf_tensor("KVBF", [128, 1024], BF16)
    E0F = nc.alloc_sbuf_tensor("E0F", [128, 128], F32)
    E0B = nc.alloc_sbuf_tensor("E0B", [128, 128], BF16)
    ONESC = nc.alloc_sbuf_tensor("ONESC", [128, 1], BF16)
    PHIQ = nc.alloc_sbuf_tensor("PHIQ", [128, 1024], BF16)
    RKQ = nc.alloc_sbuf_tensor("RKQ", [128, 200], F32)
    QMB = nc.alloc_sbuf_tensor("QMB", [128, 32], BF16)
    BQT = nc.alloc_sbuf_tensor("BQT", [128, 96], F32)
    EMKS = nc.alloc_sbuf_tensor("EMKS", [128, 8], F32)
    SCL = nc.alloc_sbuf_tensor("SCL", [128, 1], F32)
    SLTS = nc.alloc_sbuf_tensor("SLTS", [1, 8], U32)
    FLS = nc.alloc_sbuf_tensor("FLS", [1, 4], U32)
    EPSS = nc.alloc_sbuf_tensor("EPSS", [1, 2], F32)

    M0, M1 = EXG[0:1, 0:512], EXG[0:1, 512:1024]
    RV0, RV1 = EXG[0:1, 1024:1536], EXG[0:1, 1536:2048]
    W0, W1 = EXG[0:1, 2048:2560], EXG[0:1, 2560:3072]
    MROW, RVROW = [M0, M1], [RV0, RV1]
    MVIEW = [EXG[:, 0:512], EXG[:, 512:1024]]
    RVVIEW = [EXG[:, 1024:1536], EXG[:, 1536:2048]]
    W0VIEW, W1VIEW = EXG[:, 2048:2560], EXG[:, 2560:3072]
    MT256, ET256 = EXG[:, 2048:2304], EXG[:, 2304:2560]
    MT512, ET512 = EXG[:, 2048:2560], EXG[:, 2560:3072]

    PS = [nc.alloc_psum_tensor(f"PS{i}", [128, 512], F32) for i in range(3)]
    PS3 = nc.alloc_psum_tensor("PS3", [128, 512], F32)
    PSKV = nc.alloc_psum_tensor("PSKV", [128, 1024], F32)
    PSKS = nc.alloc_psum_tensor("PSKS", [128, 512], F32)

    sems = ("GW", "SY", "WFR", "MM", "EVV", "EVS", "VS", "SV", "EX", "PX1", "PX2")
    sem = {n: nc.alloc_semaphore(n) for n in sems}

    p = Plan()
    bank_guard = {}

    def gdma(dst, src, pre=()):
        tgt = p.bump("GW", 16)

        def fn(e, dst=dst, src=src, pre=tuple(pre)):
            for s_, v_ in pre:
                e.wait_ge(sem[s_], v_)
            e.dma_start(out=dst, in_=src).then_inc(sem["GW"], 16)
        p.emit("gpsimd", fn)
        return tgt

    def sdma(dst, src, pre=()):
        tgt = p.bump("SY", 16)

        def fn(e, dst=dst, src=src, pre=tuple(pre)):
            for s_, v_ in pre:
                e.wait_ge(sem[s_], v_)
            e.dma_start(out=dst, in_=src).then_inc(sem["SY"], 16)
        p.emit("sync", fn)
        return tgt

    def ewait(eng, s, v):
        p.emit(eng, lambda e, s=s, v=v: e.wait_ge(sem[s], v))

    def einc(eng, s, k=1):
        tgt = p.bump(s, k)
        p.emit(eng, lambda e, s=s, k=k: e.nop().then_inc(sem[s], k))
        return tgt

    def t_mm(ps_ap, lhsT, rhs, start, stop, waits=(), inc=None):
        if inc is not None:
            p.bump(inc[0], inc[1])

        def fn(e, ps_ap=ps_ap, lhsT=lhsT, rhs=rhs, start=start, stop=stop,
               waits=tuple(waits), inc=inc):
            for s_, v_ in waits:
                e.wait_ge(sem[s_], v_)
            ins = e.matmul(ps_ap, lhsT, rhs, start=start, stop=stop)
            if inc is not None:
                ins.then_inc(sem[inc[0]], inc[1])
        p.emit("tensor", fn)

    def t_guard(bn):
        g = bank_guard.pop(bn, None)
        return [g] if g else []

    def v_op(fb, inc=None, waits=()):
        tgt = p.bump(inc, 1) if inc else None

        def fn(e, fb=fb, waits=tuple(waits), inc=inc):
            for s_, v_ in waits:
                e.wait_ge(sem[s_], v_)
            ins = fb(e)
            if inc:
                last = ins[-1] if isinstance(ins, (list, tuple)) else ins
                last.then_inc(sem[inc], 1)
        p.emit("vector", fn)
        return tgt

    def s_op(fb, inc=None, waits=()):
        tgt = p.bump(inc, 1) if inc else None

        def fn(e, fb=fb, waits=tuple(waits), inc=inc):
            for s_, v_ in waits:
                e.wait_ge(sem[s_], v_)
            ins = fb(e)
            if inc:
                last = ins[-1] if isinstance(ins, (list, tuple)) else ins
                last.then_inc(sem[inc], 1)
        p.emit("scalar", fn)
        return tgt

    ring = {"n": 0}

    def ring_load(src_ap, width=8192, pre=()):
        n = ring["n"]
        slot = WR[:, n % 2 * 8192: n % 2 * 8192 + width]
        waits = list(pre)
        if n >= 2:
            waits.append(("WFR", n - 1))
        tgt = gdma(slot, src_ap, pre=waits)
        ring["n"] = n + 1
        return slot, tgt

    def ring_free():
        return einc("tensor", "WFR")

    rot = {"i": 0}

    def next_bank():
        i = rot["i"] % 3
        rot["i"] += 1
        return f"PS{i}", PS[i]

    def bsl(buf, c, tb):
        return buf[:, c * T1 + tb * 512: c * T1 + (tb + 1) * 512]

    def ln_stats(src_of, sq_slots, acc, keys, pers_rows=None, src_ready=None):
        res = []
        prev_dma = None
        for tb in range(NTB):
            sum_ps, sq_ps = acc[tb]
            kb_sum, kb_sq = keys[tb]
            gsum, gsq = t_guard(kb_sum), t_guard(kb_sq)
            mm_sq = {}
            for c in range(NCH):
                src = src_of(c, tb)
                slot = sq_slots[c % 2]
                w = [("MM", mm_sq[c - 2])] if c - 2 in mm_sq else []
                if c == 0 and src_ready is not None:
                    w = w + [("VS", src_ready)]
                sq_t = s_op(lambda e, s_=slot, a=src: e.activation(
                    s_, a, ACT.Square), inc="SV", waits=w)
                sw0 = (gsum if c == 0 else [])
                if c == 0 and src_ready is not None:
                    sw0 = sw0 + [("VS", src_ready)]
                t_mm(sum_ps, ONESC[:], src, start=(c == 0), stop=(c == NCH - 1),
                     waits=sw0)
                t_mm(sq_ps, ONESC[:], slot, start=(c == 0), stop=(c == NCH - 1),
                     waits=[("SV", sq_t)] + (gsq if c == 0 else []), inc=("MM", 1))
                mm_sq[c] = p.val("MM")
            mm_done = p.val("MM")

            def fin(e, tb=tb, sum_ps=sum_ps, sq_ps=sq_ps):
                e.tensor_scalar(MROW[tb], sum_ps, 1.0 / D, None, ALU.mult)
                e.tensor_mul(W0, MROW[tb], MROW[tb])
                return e.scalar_tensor_tensor(W1, sq_ps, 1.0 / D, W0,
                                              ALU.mult, ALU.subtract)
            v_f = v_op(fin, inc="VS", waits=[("MM", mm_done), ("SV", p.val("SV"))] +
                       ([("SY", prev_dma)] if prev_dma else []))
            bank_guard[kb_sum] = ("VS", v_f)
            bank_guard[kb_sq] = ("VS", v_f)
            s_q = s_op(lambda e: e.activation(W1, W1, ACT.Sqrt, bias=EPSS[0:1, 0:1]),
                       inc="SV", waits=[("VS", v_f)])
            v_r = v_op(lambda e, tb=tb: e.reciprocal(RVROW[tb], W1),
                       inc="VS", waits=[("SV", s_q)])
            if pers_rows is not None:
                m_i, s_i = pers_rows
                sdma(RWS[m_i, ts(tb, 512)], MROW[tb], pre=[("VS", v_f)])
                prev_dma = sdma(RWS[s_i, ts(tb, 512)], W1, pre=[("VS", v_r)])
            res.append(v_r)
        return res


    v_c0 = v_op(lambda e: (
        e.memset(E0F[:], 0.0), e.memset(E0B[:], 0.0),
        e.memset(ONESC[:], 1.0), e.memset(EPSS[:], EPS_LN),
        e.memset(EXG[:], 0.0), e.memset(TRI[:], 0.0))[-1], inc="VS")
    v_consts = v_op(lambda e: (
        e.memset(E0F[0:1, :], 1.0), e.memset(E0B[0:1, :], 1.0))[-1],
        inc="VS", waits=[("VS", v_c0)])

    ld_slt = sdma(SLTS[:], SLT[:])
    sdma(BQT[:, 0:32], BQD[:])
    sdma(BQT[:, 32:64], BOD[:])
    ld_bqt = sdma(BQT[:, 64:96], BFD[:])
    ld_emk = sdma(EMKS[:], EMKD[:])

    ld_bk = gdma(TRI[0:1, 0:1024], BKR[:], pre=[("VS", v_consts)])
    ld_bv = gdma(TRI[0:1, 1024:2048], BVR[:])
    ld_bcb = gdma(BCBS[:], BCBD[:])

    kv_mm_done = [0, 0]
    spill_done = [0, 0]

    for s in (0, 1):
        pre = [] if s == 0 else [("SY", spill_done[0]), ("MM", kv_mm_done[0])]
        gdma(BUF1[:, 0:16384], X[s][:, 0:16, :], pre=pre)
        x_tgt = gdma(BUF1[:, 16384:32768], X[s][:, 16:32, :])

        ewait("tensor", "GW", x_tgt)
        ewait("vector", "GW", x_tgt)
        ewait("scalar", "GW", x_tgt)
        if s == 0:
            ewait("tensor", "VS", v_consts)
        ln_stats(lambda c, tb: bsl(BUF1, c, tb),
                 (TRI[:, 2048:2560], TRI[:, 2560:3072]),
                 acc=[(PS[0][0:1, 0:512], PS[1][0:1, 0:512])] * 2,
                 keys=[("PS0", "PS1")] * 2,
                 pers_rows=(2 * s + 0, 2 * s + 1))

        for tb in range(NTB):
            g2 = t_guard("PS2")
            t_mm(PS[2][:, 0:512], E0F[:], MVIEW[tb], start=True, stop=True,
                 waits=g2 + [("VS", p.val("VS"))])
            g0 = t_guard("PS0")
            t_mm(PS[0][:, 0:512], E0F[:], RVVIEW[tb], start=True, stop=True,
                 waits=g0, inc=("MM", 1))
            bc_done = p.val("MM")
            v_cp = v_op(lambda e: (
                e.tensor_copy(TRI[:, 3072:3584], PS[2][:, 0:512]),
                e.tensor_copy(TRI[:, 3584:4096], PS[0][:, 0:512]))[-1],
                inc="VS", waits=[("MM", bc_done)])
            bank_guard["PS2"] = ("VS", v_cp)
            bank_guard["PS0"] = ("VS", v_cp)
            for c in range(NCH):
                sl = bsl(BUF1, c, tb)
                v_op(lambda e, sl=sl: (
                    e.tensor_sub(sl, sl, TRI[:, 3072:3584]),
                    e.tensor_mul(sl, sl, TRI[:, 3584:4096]))[-1],
                    inc="VS", waits=([("VS", v_cp)] if c == 0 else []))
        v_norm = p.val("VS")

        sdma(XNSP[s][:, 0:16384], BUF1[:, 0:16384], pre=[("VS", v_norm)])
        spill_done[s] = sdma(XNSP[s][:, 16384:32768], BUF1[:, 16384:32768])

        KVW = [WR[:, 0:16384], BUF2[:, 16384:32768]]
        for wb in range(4):
            src = WK[wb] if wb < 2 else WV[wb - 2]
            n = ring["n"]
            slot = KVW[n % 2]
            wwaits = [("WFR", n - 1)] if n >= 2 else []
            wtgt = gdma(slot, src, pre=wwaits)
            ring["n"] = n + 1
            carrier = TRI[:, (wb % 2) * 512:(wb % 2) * 512 + 512] if wb < 2 else \
                TRI[:, 1024 + (wb % 2) * 512:1024 + (wb % 2) * 512 + 512]
            ewait("tensor", "GW", wtgt)
            if wb == 0:
                ewait("tensor", "GW", max(ld_bk, ld_bv))
                ewait("tensor", "VS", v_norm)
            for tile in range(8):
                bn, bap = next_bank()
                pap = bap[:, 0:512]
                g = t_guard(bn)
                for c in range(NCH):
                    t_mm(pap, BUF1[:, c * T1 + tile * 128: c * T1 + tile * 128 + 128],
                         slot[:, ts(c, 512)], start=(c == 0), stop=False,
                         waits=(g if c == 0 else []))
                t_mm(pap, E0B[:], carrier, start=False, stop=True, inc=("MM", 1))
                mm_t = p.val("MM")
                dc = tile * 1024 + (wb % 2) * 512
                extra = [("MM", kv_mm_done[0])] if (s == 1 and wb == 0 and tile == 0) else []
                if wb < 2:
                    vm = v_op(lambda e, pap=pap: e.tensor_single_scalar(
                        MT512, pap, 0.0, ALU.min), inc="VS",
                        waits=[("MM", mm_t), ("SV", p.val("SV"))] + extra)
                    se = s_op(lambda e: e.activation(ET512, MT512, ACT.Exp),
                              inc="SV", waits=[("VS", vm), ("EVV", p.val("EVV"))])
                    vp = v_op(lambda e, pap=pap, dc=dc: e.scalar_tensor_tensor(
                        PK[:, dc:dc + 512], pap, 0.0, ET512, ALU.max, ALU.add),
                        inc="EVV", waits=[("SV", se)])
                else:
                    vp = v_op(lambda e, pap=pap, dc=dc: e.tensor_copy(
                        PV[:, dc:dc + 512], pap), inc="EVV",
                        waits=[("MM", mm_t)] + extra)
                bank_guard[bn] = ("EVV", vp)
            ring_free()
        kv_evv = p.val("EVV")

        ewait("tensor", "EVV", kv_evv)
        for tile in range(8):
            for h in range(8):
                kb = PK[:, tile * 1024 + h * 128: tile * 1024 + (h + 1) * 128]
                vb = PV[:, tile * 1024 + h * 128: tile * 1024 + (h + 1) * 128]
                t_mm(PSKV[:, ts(h, 128)], kb, vb,
                     start=(s == 0 and tile == 0 and h in (0, 4)),
                     stop=(s == 1 and tile == 7 and h in (3, 7)))
                t_mm(PSKS[:, h:h + 1], kb, ONESC[:],
                     start=(s == 0 and tile == 0 and h == 0),
                     stop=(s == 1 and tile == 7 and h == 7),
                     inc=(("MM", 1) if (tile == 7 and h == 7) else None))
        kv_mm_done[s] = p.val("MM")

    exg_my = EXG[:, 0:1032]
    exg_oth = EXG[:, 1040:2072]
    v_kvst = v_op(lambda e: (
        e.tensor_copy(exg_my[:, 0:1024], PSKV[:]),
        e.tensor_copy(exg_my[:, 1024:1032], PSKS[:, 0:8]))[-1],
        inc="EVV", waits=[("MM", kv_mm_done[1])])
    bank_guard["KVA"] = ("EVV", v_kvst)
    bank_guard["KVB"] = ("EVV", v_kvst)
    bank_guard["KSA"] = ("EVV", v_kvst)

    def emit_exchange(D_, F_, my_pay, oth_pay, nonce_word, pxname, pre=()):
        def fn(e, pre=tuple(pre)):
            for s_, v_ in pre:
                e.wait_ge(sem[s_], v_)
            rmy = e.to_reg(e.value_load(SLTS[0:1, 0:1]))
            rnc = e.to_reg(e.value_load(SLTS[0:1, nonce_word:nonce_word + 1]))

            def one(my, oth):
                tag = f"{pxname}_{my}"
                e.dma_start(out=D_[my], in_=my_pay).then_inc(sem[pxname], 16)
                e.wait_ge(sem[pxname], 16)
                e.dma_start(out=F_[my], in_=SLTS[0:1, 0:4]).then_inc(sem[pxname], 16)
                with e.register("rc_" + tag) as rc, e.register("rn_" + tag) as rn:
                    e.reg_mov(rc, 1)
                    e.reg_mov(rn, 32)
                    with e.While(rc):
                        e.dma_start(out=FLS[:], in_=F_[oth]).then_inc(sem[pxname], 16)
                        e.reg_add(rn, rn, 16)
                        e.wait_ge(sem[pxname], rn)
                        rf = e.value_load(FLS[0:1, nonce_word:nonce_word + 1])
                        e.reg_alu(rc, rf, rnc, ALU.not_equal)
                    e.dma_start(out=oth_pay, in_=D_[oth]).then_inc(sem[pxname], 16)
                    e.reg_add(rn, rn, 16)
                    e.wait_ge(sem[pxname], rn)
            with e.If_eq(rmy, 0):
                one(0, 1)
            with e.Else():
                one(1, 0)
            e.nop().then_inc(sem["EX"], 1)
        p.emit("sync", fn)
        return p.bump("EX", 1)

    ex1 = emit_exchange(SH1D, SH1F, exg_my, exg_oth, 1, "PX1",
                        pre=[("EVV", v_kvst), ("SY", ld_slt)])

    def kv_combine(e):
        e.tensor_add(KVBF[:], exg_my[:, 0:1024], exg_oth[:, 0:1024])
        e.tensor_add(RKQ[:, 0:8], exg_my[:, 1024:1032], exg_oth[:, 1024:1032])
        e.tensor_scalar(RKQ[:, 0:8], RKQ[:, 0:8], 1e-10, None, ALU.add)
        ins = e.reciprocal(RKQ[:, 0:8], RKQ[:, 0:8])
        for _ in range(6):
            ins = e.tensor_copy(TRI[:, 3584:4096], BCBS[:, 0:512])
        return ins
    v_kv_ready = v_op(kv_combine, inc="VS", waits=[("EX", ex1)])

    router_done = {}
    rering_guard = {k: ("MM", kv_mm_done[1]) for k in range(4)}

    def phase_b(s, gate_vs=None, gate_evs_buf2=None):
        if s == 0:
            b1 = sdma(BUF1[:, 0:16384], XNSP[0][:, 0:16384],
                      pre=[("VS", gate_vs)])
            b2 = sdma(BUF1[:, 16384:32768], XNSP[0][:, 16384:32768])
            ewait("tensor", "SY", b2)

        phiq_free = p.val("MM")
        for om in range(16):
            slot, wtgt = ring_load(WQ[om])
            ewait("tensor", "GW", wtgt)
            for m in range(2):
                h = om * 2 + m
                kvh = h // 4
                banks = [next_bank(), next_bank()]
                gq = t_guard(banks[0][0]) + t_guard(banks[1][0])
                for c in range(NCH):
                    lhs = slot[:, c * 256 + m * 128: c * 256 + (m + 1) * 128]
                    for tb in range(NTB):
                        t_mm(banks[tb][1][:, 0:512], lhs, bsl(BUF1, c, tb),
                             start=(c == 0), stop=(c == NCH - 1),
                             waits=(gq if (c == 0 and tb == 0) else []),
                             inc=(("MM", 1) if (c == NCH - 1 and tb == 1) else None))
                mm_q = p.val("MM")
                for tb in range(NTB):
                    bn, bap = banks[tb]
                    vm = v_op(lambda e, bap=bap, h=h: e.tensor_scalar(
                        MT512, bap[:, 0:512], BQT[:, h:h + 1], 0.0, ALU.add, ALU.min),
                        inc="VS", waits=[("MM", mm_q), ("MM", phiq_free),
                                         ("SV", p.val("SV"))])
                    se = s_op(lambda e: e.activation(ET512, MT512, ACT.Exp),
                              inc="SV", waits=[("VS", vm), ("EVV", p.val("EVV"))])
                    sr = s_op(lambda e, bap=bap, h=h: e.activation(
                        bap[:, 0:512], bap[:, 0:512], ACT.Relu,
                        bias=BQT[:, h:h + 1]), inc="SV")
                    vp = v_op(lambda e, bap=bap, tb=tb: e.tensor_add(
                        PHIQ[:, ts(tb, 512)], bap[:, 0:512], ET512),
                        inc="EVV", waits=[("SV", sr)])
                    bank_guard[bn] = ("EVV", vp)
                for tb in range(NTB):
                    bn2, bap2 = next_bank()
                    g2 = t_guard(bn2)
                    t_mm(bap2[:, 0:512], KVBF[:, ts(kvh, 128)], PHIQ[:, ts(tb, 512)],
                         start=True, stop=True,
                         waits=g2 + [("EVV", p.val("EVV")), ("VS", v_kv_ready)],
                         inc=("MM", 1))
                    mm_a = p.val("MM")
                    ew = [("MM", mm_a)]
                    if gate_evs_buf2 is not None and h == 0 and tb == 0:
                        ew.append(("VS", gate_evs_buf2))
                    sa = v_op(lambda e, bap2=bap2, h=h, tb=tb, kvh=kvh: e.tensor_scalar(
                        bsl(BUF2, h, tb), bap2[:, 0:512], RKQ[:, kvh:kvh + 1], None,
                        ALU.mult),
                        inc="EVV", waits=ew)
                    bank_guard[bn2] = ("EVV", sa)
                phiq_free = p.val("MM")
            ring_free()

        ewait("tensor", "EVV", p.val("EVV"))
        ewait("tensor", "EVS", p.val("EVS"))
        sq_mm = {}
        for om in range(16):
            slot, wtgt = ring_load(WO[om])
            ewait("tensor", "GW", wtgt)
            for m in range(2):
                oc = om * 2 + m
                banks = [next_bank(), next_bank()]
                gq = t_guard(banks[0][0]) + t_guard(banks[1][0])
                for c in range(NCH):
                    lhs = slot[:, c * 256 + m * 128: c * 256 + (m + 1) * 128]
                    for tb in range(NTB):
                        t_mm(banks[tb][1][:, 0:512], lhs, bsl(BUF2, c, tb),
                             start=(c == 0), stop=(c == NCH - 1),
                             waits=(gq if (c == 0 and tb == 0) else []),
                             inc=(("MM", 1) if (c == NCH - 1 and tb == 1) else None))
                mm_o = p.val("MM")
                for tb in range(NTB):
                    bn, bap = banks[tb]
                    sqsl = TRI[:, 2048 + ((2 * oc + tb) % 2) * 512:
                               2560 + ((2 * oc + tb) % 2) * 512]
                    sw = [("MM", mm_o)]
                    if (oc - 1, tb) in sq_mm:
                        sw.append(("MM", sq_mm[(oc - 1, tb)]))
                    so = s_op(lambda e, bap=bap, oc=oc, tb=tb, sqsl=sqsl: (
                        e.activation(bsl(BUF1, oc, tb), bap[:, 0:512], ACT.Identity,
                                     bias=BQT[:, 32 + oc:33 + oc]),
                        e.activation(sqsl, bap[:, 0:512], ACT.Square,
                                     bias=BQT[:, 32 + oc:33 + oc]))[-1],
                        inc="EVS", waits=sw)
                    bank_guard[bn] = ("EVS", so)
                    sum_ps = PSKV[0:1, ts(tb, 512)]
                    sq_ps = PSKS[0:1, 0:512] if tb == 0 else PS3[0:1, 0:512]
                    kb_sum = "KVA" if tb == 0 else "KVB"
                    kb_sq = "KSA" if tb == 0 else "PS3"
                    t_mm(sum_ps, ONESC[:], bsl(BUF1, oc, tb),
                         start=(oc == 0), stop=(oc == 31),
                         waits=[("EVS", so)] + (t_guard(kb_sum) if oc == 0 else []))
                    t_mm(sq_ps, ONESC[:], sqsl, start=(oc == 0), stop=(oc == 31),
                         waits=(t_guard(kb_sq) if oc == 0 else []), inc=("MM", 1))
                    sq_mm[(oc, tb)] = p.val("MM")
            ring_free()
        mm_ost = p.val("MM")

        fin = []
        for tb in range(NTB):
            sum_ps = PSKV[0:1, ts(tb, 512)]
            sq_ps = PSKS[0:1, 0:512] if tb == 0 else PS3[0:1, 0:512]

            def fo(e, tb=tb, sum_ps=sum_ps, sq_ps=sq_ps):
                e.tensor_scalar(MROW[tb], sum_ps, 1.0 / D, None, ALU.mult)
                e.tensor_mul(W0, MROW[tb], MROW[tb])
                return e.scalar_tensor_tensor(W1, sq_ps, 1.0 / D, W0,
                                              ALU.mult, ALU.subtract)
            v_f = v_op(fo, inc="VS", waits=[("MM", mm_ost), ("EX", ex1),
                                            ("SV", p.val("SV"))])
            s_q = s_op(lambda e: e.activation(W1, W1, ACT.Sqrt, bias=EPSS[0:1, 0:1]),
                       inc="SV", waits=[("VS", v_f)])
            v_r = v_op(lambda e, tb=tb: e.reciprocal(RVROW[tb], W1),
                       inc="VS", waits=[("SV", s_q)])
            ldm = sdma(W0, RWS[2 * s + 0, ts(tb, 512)], pre=[("VS", v_r)])
            v_cc = v_op(lambda e, tb=tb: (
                e.tensor_mul(MROW[tb], MROW[tb], RVROW[tb]),
                e.scalar_tensor_tensor(MROW[tb], MROW[tb], -1.0, W0,
                                       ALU.mult, ALU.add))[-1],
                inc="VS", waits=[("SY", ldm)])
            fin.append(v_cc)
        bank_guard["KVA"] = ("VS", fin[0])
        bank_guard["KSA"] = ("VS", fin[0])
        bank_guard["KVB"] = ("VS", fin[1])
        bank_guard["PS3"] = ("VS", fin[1])

        rering = [TRI[:, i * 512:(i + 1) * 512] for i in range(4)]
        bcA, bcB, bcC = TRI[:, 2048:2560], TRI[:, 2560:3072], TRI[:, 3072:3584]
        TMPB = TRI[:, 3584:4096]
        v_x1_last = 0
        prev_bc_mm = [0]
        for tb in range(NTB):
            ldsa = sdma(W1, RWS[2 * s + 1, ts(tb, 512)],
                        pre=[("VS", fin[tb]), ("MM", prev_bc_mm[0])])
            bn, bap = next_bank()
            t_mm(bap[:, 0:512], E0F[:], RVVIEW[tb], start=True, stop=True,
                 waits=t_guard(bn) + [("VS", fin[tb])])
            bn2, bap2 = next_bank()
            t_mm(bap2[:, 0:512], E0F[:], MVIEW[tb], start=True, stop=True,
                 waits=t_guard(bn2))
            bn3, bap3 = next_bank()
            t_mm(bap3[:, 0:512], E0F[:], W1VIEW, start=True, stop=True,
                 waits=t_guard(bn3) + [("SY", ldsa)], inc=("MM", 1))
            mm_bc = p.val("MM")
            prev_bc_mm[0] = mm_bc
            v_cp = v_op(lambda e, bap=bap, bap2=bap2, bap3=bap3: (
                e.tensor_copy(bcA, bap[:, 0:512]),
                e.tensor_copy(bcB, bap2[:, 0:512]),
                e.tensor_copy(bcC, bap3[:, 0:512]))[-1],
                inc="VS", waits=[("MM", mm_bc)])
            bank_guard[bn] = ("VS", v_cp)
            bank_guard[bn2] = ("VS", v_cp)
            bank_guard[bn3] = ("VS", v_cp)
            for c in range(NCH):
                key = c % 4
                prew = [rering_guard[key]] if rering_guard[key] else []
                ldx = sdma(rering[key],
                           XNSP[s][:, c * T1 + tb * 512: c * T1 + (tb + 1) * 512],
                           pre=prew)
                vv = v_op(lambda e, c=c, tb=tb, key=key: (
                    e.tensor_mul(TMPB, bsl(BUF1, c, tb), bcA),
                    e.tensor_add(TMPB, TMPB, bcB),
                    e.tensor_mul(rering[key], rering[key], bcC),
                    e.tensor_add(bsl(BUF2, c, tb), TMPB, rering[key]))[-1],
                    inc="VS",
                    waits=[("VS", v_cp), ("SY", ldx)] if c == 0 else [("SY", ldx)])
                rering_guard[key] = ("VS", vv)
            v_x1_last = p.val("VS")

        ln_stats(lambda c, tb: bsl(BUF2, c, tb),
                 (TRI[:, 2048:2560], TRI[:, 2560:3072]),
                 acc=[(PSKV[0:1, 0:512], PSKS[0:1, 0:512]),
                      (PSKV[0:1, 512:1024], PS3[0:1, 0:512])],
                 keys=[("KVA", "KSA"), ("KVB", "PS3")],
                 pers_rows=((4, 5) if s == 0 else None),
                 src_ready=v_x1_last)

        rs_col = RKQ[:, 8:40] if s == 1 else RKQ[:, 40:72]
        rs2_col = RKQ[:, 168:200]
        for tb in range(NTB):
            ldmsk = sdma(W1[0:1, 0:512], MSK[s][:, ts(tb, 512)],
                         pre=[("VS", p.val("VS"))])
            bn, bap = next_bank()
            t_mm(bap[:, 0:512], E0F[:], MVIEW[tb], start=True, stop=True,
                 waits=t_guard(bn) + [("VS", p.val("VS"))])
            bn2, bap2 = next_bank()
            t_mm(bap2[:, 0:512], E0F[:], RVVIEW[tb], start=True, stop=True,
                 waits=t_guard(bn2))
            bn3, bap3 = next_bank()
            t_mm(bap3[:, 0:512], E0F[:], W1VIEW, start=True, stop=True,
                 waits=t_guard(bn3) + [("SY", ldmsk)], inc=("MM", 1))
            mm_bc = p.val("MM")
            v_cp = v_op(lambda e, bap=bap, bap2=bap2, bap3=bap3: (
                e.tensor_copy(bcA, bap[:, 0:512]),
                e.tensor_copy(bcB, bap2[:, 0:512]),
                e.tensor_copy(bcC, bap3[:, 0:512]))[-1],
                inc="VS", waits=[("MM", mm_bc)])
            bank_guard[bn] = ("VS", v_cp)
            bank_guard[bn2] = ("VS", v_cp)
            bank_guard[bn3] = ("VS", v_cp)
            for c in range(NCH):
                sl = bsl(BUF2, c, tb)
                v_op(lambda e, sl=sl: (
                    e.tensor_sub(sl, sl, bcA),
                    e.tensor_mul(sl, sl, bcB))[-1],
                    inc="VS", waits=([("VS", v_cp)] if c == 0 else []))
                v_op(lambda e, sl=sl: e.tensor_mul(TMPB, sl, bcC), inc="VS")
                dst = rs_col[:, c:c + 1] if tb == 0 else rs2_col[:, c:c + 1]
                v_op(lambda e, dst=dst: e.reduce_sum(
                    dst, TMPB, mybir.AxisListType.X), inc="VS")
        v_op(lambda e, rs_col=rs_col: e.tensor_add(rs_col, rs_col, rs2_col),
             inc="VS")
        router_done[s] = p.val("VS")
        return v_x1_last

    v_x1_s1 = phase_b(1)
    v_x1_s0 = phase_b(0, gate_vs=v_x1_s1, gate_evs_buf2=router_done[1])

    v_pay = v_op(lambda e: e.tensor_add(RKQ[:, 72:104], RKQ[:, 8:40], RKQ[:, 40:72]),
                 inc="VS", waits=[("VS", router_done[0])])
    ex2 = emit_exchange(SH2D, SH2F, RKQ[:, 72:104], RKQ[:, 104:136], 2, "PX2",
                        pre=[("EX", ex1), ("VS", v_pay)])
    def qm_fold(e):
        ins = e.tensor_add(RKQ[:, 8:40], RKQ[:, 72:104], RKQ[:, 104:136])
        for _ in range(6):
            ins = e.tensor_copy(TRI[:, 3584:4096], BCBS[:, 0:512])
        return ins
    v_qm = v_op(qm_fold, inc="VS", waits=[("EX", ex2)])

    rw_slot, rw_tgt = ring_load(RWTD[:], width=256)
    PROD = TRI[:, 0:256]
    v_prod = None
    for c in range(NCH):
        w = [("VS", v_qm), ("GW", rw_tgt)] if c == 0 else []
        v_prod = v_op(lambda e, c=c: e.tensor_scalar(
            PROD[:, ts(c, 8)], rw_slot[:, ts(c, 8)], RKQ[:, 8 + c:9 + c], None,
            ALU.mult), inc="VS", waits=w)
    g = t_guard("PS0")
    for c in range(NCH):
        t_mm(PS[0][0:1, 0:8], ONESC[:], PROD[:, ts(c, 8)],
             start=(c == 0), stop=(c == NCH - 1),
             waits=(g + [("VS", v_prod)] if c == 0 else []),
             inc=(("MM", 1) if c == NCH - 1 else None))
    mm_lg = p.val("MM")
    ring_free()
    ld_rb = sdma(W1[0:1, 0:8], RBR[:], pre=[])

    v_sm1 = v_op(lambda e: e.tensor_add(W0[0:1, 0:8], PS[0][0:1, 0:8],
                                        W1[0:1, 0:8]),
                 inc="VS", waits=[("MM", mm_lg), ("SY", ld_rb)])
    bank_guard["PS0"] = ("VS", v_sm1)
    if DBG:
        sdma(RWS[6, 0:256], EXG[0:1, 2304:2560], pre=[("VS", v_sm1)])
        sdma(RWS[7, 0:8], W0[0:1, 0:8])
    g = t_guard("PS1")
    t_mm(PS[1][:, 0:8], E0F[:], W0VIEW[:, 0:8], start=True, stop=True,
         waits=g + [("VS", v_sm1)], inc=("MM", 1))
    mm_rbc = p.val("MM")
    s_exp = s_op(lambda e: e.activation(RKQ[:, 144:152], PS[1][:, 0:8], ACT.Exp),
                 inc="SV", waits=[("MM", mm_rbc)])
    def scl_fold(e):
        def age(n=4):
            for _ in range(n):
                e.tensor_copy(TRI[:, 3584:4096], BCBS[:, 0:512])
        e.reduce_sum(RKQ[:, 152:153], RKQ[:, 144:152], mybir.AxisListType.X)
        age()
        e.reciprocal(RKQ[:, 153:154], RKQ[:, 152:153])
        e.tensor_mul(RKQ[:, 136:144], RKQ[:, 144:152], EMKS[:])
        age()
        e.reduce_sum(RKQ[:, 154:155], RKQ[:, 136:144], mybir.AxisListType.X)
        age()
        ins = e.tensor_mul(SCL[:], RKQ[:, 154:155], RKQ[:, 153:154])
        age(6)
        return e.tensor_copy(TRI[:, 3584:4096], BCBS[:, 0:512])
    v_scl = v_op(scl_fold, inc="VS", waits=[("SV", s_exp), ("SY", ld_emk)])
    bank_guard["PS1"] = ("VS", v_scl)

    ac_slot, ac_tgt = ring_load(ACBD[:], width=4096)
    ewait("tensor", "GW", ac_tgt)
    ewait("tensor", "VS", router_done[0])
    tsc = [TRI[:, 0:512], TRI[:, 512:1024]]
    for tb in range(NTB):
        bn, bap = next_bank()
        g = t_guard(bn)
        for c in range(NCH):
            t_mm(bap[:, 0:512], ac_slot[:, ts(c, 128)], bsl(BUF2, c, tb),
                 start=(c == 0), stop=(c == NCH - 1),
                 waits=(g if c == 0 else []), inc=(("MM", 1) if c == NCH - 1 else None))
        mm_t = p.val("MM")
        vt = v_op(lambda e, bap=bap, tb=tb: e.tensor_scalar(
            tsc[tb], bap[:, 0:512], SCL[:, 0:1], None, ALU.mult),
            inc="EVV", waits=[("MM", mm_t), ("VS", v_scl)])
        bank_guard[bn] = ("EVV", vt)
    ring_free()
    evv_t = p.val("EVV")

    ewait("tensor", "GW", ld_bcb)
    ewait("tensor", "EVV", evv_t)
    sq_mm = {}
    for om in range(16):
        slot, wtgt = ring_load(WFD[om])
        ewait("tensor", "GW", wtgt)
        for m in range(2):
            oc = om * 2 + m
            banks = [next_bank(), next_bank()]
            gq = t_guard(banks[0][0]) + t_guard(banks[1][0])
            for c in range(NCH):
                lhs = slot[:, c * 256 + m * 128: c * 256 + (m + 1) * 128]
                for tb in range(NTB):
                    t_mm(banks[tb][1][:, 0:512], lhs, bsl(BUF2, c, tb),
                         start=(c == 0), stop=False,
                         waits=(gq if (c == 0 and tb == 0) else []))
            for tb in range(NTB):
                t_mm(banks[tb][1][:, 0:512], BCBS[:, ts(oc, 128)], tsc[tb],
                     start=False, stop=True,
                     inc=(("MM", 1) if tb == 1 else None))
            mm_o = p.val("MM")
            for tb in range(NTB):
                bn, bap = banks[tb]
                sqsl = TRI[:, 2048 + ((2 * oc + tb) % 2) * 512:
                           2560 + ((2 * oc + tb) % 2) * 512]
                sw = [("MM", mm_o)]
                if (oc - 1, tb) in sq_mm:
                    sw.append(("MM", sq_mm[(oc - 1, tb)]))
                so = s_op(lambda e, bap=bap, oc=oc, tb=tb, sqsl=sqsl: (
                    e.activation(bsl(BUF1, oc, tb), bap[:, 0:512], ACT.Identity,
                                 bias=BQT[:, 64 + oc:65 + oc]),
                    e.activation(sqsl, bap[:, 0:512], ACT.Square,
                                 bias=BQT[:, 64 + oc:65 + oc]))[-1],
                    inc="EVS", waits=sw)
                bank_guard[bn] = ("EVS", so)
                sum_ps = PSKV[0:1, ts(tb, 512)]
                sq_ps = PSKS[0:1, 0:512] if tb == 0 else PS3[0:1, 0:512]
                kb_sum = "KVA" if tb == 0 else "KVB"
                kb_sq = "KSA" if tb == 0 else "PS3"
                t_mm(sum_ps, ONESC[:], bsl(BUF1, oc, tb),
                     start=(oc == 0), stop=(oc == 31),
                     waits=[("EVS", so)] + (t_guard(kb_sum) if oc == 0 else []))
                t_mm(sq_ps, ONESC[:], sqsl, start=(oc == 0), stop=(oc == 31),
                     waits=(t_guard(kb_sq) if oc == 0 else []), inc=("MM", 1))
                sq_mm[(oc, tb)] = p.val("MM")
        ring_free()
    mm_o2 = p.val("MM")

    fin2 = []
    for tb in range(NTB):
        sum_ps = PSKV[0:1, ts(tb, 512)]
        sq_ps = PSKS[0:1, 0:512] if tb == 0 else PS3[0:1, 0:512]

        def fo2(e, tb=tb, sum_ps=sum_ps, sq_ps=sq_ps):
            e.tensor_scalar(MROW[tb], sum_ps, 1.0 / D, None, ALU.mult)
            e.tensor_mul(W0, MROW[tb], MROW[tb])
            return e.scalar_tensor_tensor(W1, sq_ps, 1.0 / D, W0, ALU.mult, ALU.subtract)
        v_f = v_op(fo2, inc="VS", waits=[("MM", mm_o2), ("SV", p.val("SV"))])
        s_q = s_op(lambda e: e.activation(W1, W1, ACT.Sqrt, bias=EPSS[0:1, 0:1]),
                   inc="SV", waits=[("VS", v_f)])
        v_r = v_op(lambda e, tb=tb: e.reciprocal(RVROW[tb], W1), inc="VS",
                   waits=[("SV", s_q)])
        ldm1 = sdma(W0, RWS[4, ts(tb, 512)], pre=[("VS", v_r)])
        v_cc = v_op(lambda e, tb=tb: (
            e.tensor_mul(MROW[tb], MROW[tb], RVROW[tb]),
            e.scalar_tensor_tensor(MROW[tb], MROW[tb], -1.0, W0, ALU.mult, ALU.add))[-1],
            inc="VS", waits=[("SY", ldm1)])
        fin2.append(v_cc)
    bank_guard["KVA"] = ("VS", fin2[0])
    bank_guard["KSA"] = ("VS", fin2[0])
    bank_guard["KVB"] = ("VS", fin2[1])
    bank_guard["PS3"] = ("VS", fin2[1])

    bc_banks = {}
    bank_names = ["PS0", "PS1", "PS2", "KVA", "KVB", "KSA"]
    bank_aps = [PS[0][:, 0:512], PS[1][:, 0:512], PS[2][:, 0:512],
                PSKV[:, 0:512], PSKV[:, 512:1024], PSKS[:, 0:512]]
    ldx1 = [0, 0]
    prev_bcf = [0]
    for tb in range(NTB):
        lds1 = sdma(W1, RWS[5, ts(tb, 512)],
                    pre=[("VS", fin2[tb]), ("MM", prev_bcf[0])])
        ldx1[tb] = lds1
        i0 = tb * 3
        t_mm(bank_aps[i0], E0F[:], RVVIEW[tb], start=True, stop=True,
             waits=t_guard(bank_names[i0]) + [("VS", fin2[tb])])
        t_mm(bank_aps[i0 + 1], E0F[:], MVIEW[tb], start=True, stop=True,
             waits=t_guard(bank_names[i0 + 1]))
        t_mm(bank_aps[i0 + 2], E0F[:], W1VIEW, start=True, stop=True,
             waits=t_guard(bank_names[i0 + 2]) + [("SY", lds1)],
             inc=("MM", 1))
        prev_bcf[0] = p.val("MM")
    mm_bcf = p.val("MM")

    BCR = [TRI[:, i * 512:(i + 1) * 512] for i in range(6)]
    TMPF = TRI[:, 3072:3584]
    v_op(lambda e: tuple(e.tensor_copy(BCR[i], bank_aps[i]) for i in range(6))[-1],
         inc="VS", waits=[("MM", mm_bcf)])
    for c in range(NCH):
        for tb in range(NTB):
            slot = bsl(BUF2, c, tb)
            vv = v_op(lambda e, c=c, tb=tb, slot=slot: (
                e.tensor_mul(TMPF, bsl(BUF1, c, tb), BCR[tb * 3]),
                e.tensor_add(TMPF, TMPF, BCR[tb * 3 + 1]),
                e.tensor_mul(slot, slot, BCR[tb * 3 + 2]),
                e.tensor_add(slot, slot, TMPF))[-1],
                inc="VS")
            gdma(OUT[c][:, ts(tb, 512)], slot, pre=[("VS", p.val("VS"))])
    if DBG:
        DXN = nc.dram_tensor("DXN", [2, 128, NCH * T1], BF16, kind="ExternalOutput")
        DRW = nc.dram_tensor("DRW", [8, 1024], F32, kind="ExternalOutput")
        DS1 = nc.dram_tensor("DS1", [2, 128, 1032], F32, kind="ExternalOutput")
        DB1 = nc.dram_tensor("DB1", [128, NCH * T1], BF16, kind="ExternalOutput")
        DB2 = nc.dram_tensor("DB2", [128, NCH * T1], BF16, kind="ExternalOutput")
        DKV = nc.dram_tensor("DKV", [128, 1024], BF16, kind="ExternalOutput")
        DRK = nc.dram_tensor("DRK", [128, 200], F32, kind="ExternalOutput")
        DEX = nc.dram_tensor("DEX", [128, 3072], F32, kind="ExternalOutput")
        v_dsc = v_op(lambda e: e.tensor_copy(RKQ[:, 160:161], SCL[:]), inc="VS")
        vend = p.val("VS")
        pre = [("VS", vend), ("SY", p.val("SY"))]
        sdma(DXN[:], XNSP[:], pre=pre)
        sdma(DRW[:], RWS[:])
        sdma(DS1[:], SH1D[:])
        sdma(DB1[:], BUF1[:])
        sdma(DB2[:], BUF2[:])
        sdma(DKV[:], KVBF[:])
        sdma(DRK[:], RKQ[:])
        sdma(DEX[:], EXG[:])
    final_sy = p.val("SY")
    final_gw = p.val("GW")

    p.emit("sync", lambda e: e.wait_ge(sem["SY"], final_sy))
    p.emit("sync", lambda e: e.wait_ge(sem["GW"], final_gw))

    with nc.Block() as block:
        @block.sync
        def _(e):
            for fn in p.q["sync"]:
                fn(e)

        @block.gpsimd
        def _(e):
            for fn in p.q["gpsimd"]:
                fn(e)

        @block.vector
        def _(e):
            for fn in p.q["vector"]:
                fn(e)

        @block.scalar
        def _(e):
            for fn in p.q["scalar"]:
                fn(e)

        @block.tensor
        def _(e):
            for fn in p.q["tensor"]:
                fn(e)

    return nc



def _blk_weight(w):
    O = w.shape[0]
    return np.ascontiguousarray(
        w.reshape(O // 256, 256, NCH, 128).transpose(0, 3, 2, 1))


def prepare_in_maps(L0_attn_hidden_states, L0_attn_Wq, L0_attn_bq, L0_attn_Wk,
                    L0_attn_bk, L0_attn_Wv, L0_attn_bv, L0_attn_Wo, L0_attn_bo,
                    L0_ffn_W, L0_ffn_b, L0_ffn_A, L0_ffn_B, L0_ffn_router_W,
                    L0_ffn_router_b):
    x = np.asarray(L0_attn_hidden_states)
    Wq = np.asarray(L0_attn_Wq); bq = np.asarray(L0_attn_bq)
    Wk = np.asarray(L0_attn_Wk); bk = np.asarray(L0_attn_bk)
    Wv = np.asarray(L0_attn_Wv); bv = np.asarray(L0_attn_bv)
    Wo = np.asarray(L0_attn_Wo); bo = np.asarray(L0_attn_bo)
    Wf = np.asarray(L0_ffn_W); bf = np.asarray(L0_ffn_b)
    A = np.asarray(L0_ffn_A); B = np.asarray(L0_ffn_B)
    rW = np.asarray(L0_ffn_router_W); rb = np.asarray(L0_ffn_router_b)

    WKb = np.ascontiguousarray(
        Wk.reshape(2, 512, NCH, 128).transpose(0, 3, 2, 1))
    WVb = np.ascontiguousarray(
        Wv.reshape(2, 512, NCH, 128).transpose(0, 3, 2, 1))
    WQb = _blk_weight(Wq)
    WOb = _blk_weight(Wo)
    WFb = _blk_weight(Wf)
    ACb = np.ascontiguousarray(
        A.reshape(128, D).T.reshape(NCH, 128, 128).transpose(1, 0, 2).reshape(128, NCH * 128))
    BCb = np.ascontiguousarray(B.transpose(0, 2, 1).reshape(128, D))
    RWb = np.ascontiguousarray(
        rW.T.reshape(NCH, 128, 8).transpose(1, 0, 2).reshape(128, NCH * 8))
    BQb = np.ascontiguousarray(bq.reshape(32, 128).T)
    BOb = np.ascontiguousarray(bo.reshape(32, 128).T)
    BFb = np.ascontiguousarray(bf.reshape(32, 128).T)
    BKr = np.ascontiguousarray(bk.reshape(1, 1024))
    BVr = np.ascontiguousarray(bv.reshape(1, 1024))
    RBr = np.ascontiguousarray(rb.reshape(1, 8))
    EMk = np.zeros((128, 8), dtype=np.float32)
    for pp in range(128):
        EMk[pp, pp // 16] = 2.0
    nonce1 = int(np.random.randint(1 << 20, 1 << 30))
    nonce2 = int(np.random.randint(1 << 20, 1 << 30))

    in_maps = []
    for core in range(8):
        b = core // 4
        l = LMAP[core % 4]
        tok = (np.arange(l * 1024, l * 1024 + 2048)) % 4096
        xt = x[b, tok, :]
        Xc = np.ascontiguousarray(
            xt.reshape(2, 1024, NCH, 128).transpose(0, 3, 2, 1))
        msk = np.zeros((2, 1, T1), dtype=np.float32)
        inr = (tok >= Q_START) & (tok < Q_END_EXCL)
        msk[0, 0, :] = inr[:1024] / RCOUNT
        msk[1, 0, :] = inr[1024:] / RCOUNT
        slt = np.array([[core % 2, nonce1, nonce2, 0, 0, 0, 0, 0]], dtype=np.uint32)
        in_maps.append({
            "X": Xc, "WK": WKb, "WV": WVb, "WQ": WQb, "WO": WOb, "WFD": WFb,
            "ACBD": ACb, "BCBD": BCb, "RWTD": RWb, "BQD": BQb, "BOD": BOb,
            "BFD": BFb, "BKR": BKr, "BVR": BVr, "RBR": RBr, "MSK": msk,
            "EMKD": EMk, "SLT": slt,
        })

    return in_maps


def assemble_out(outs):
    out = np.empty((2, 4096, D), dtype=np.float32)
    for core in range(8):
        b = core // 4
        l = LMAP[core % 4]
        oc = np.asarray(outs[core])
        out[b, l * 1024:(l + 1) * 1024, :] = oc.reshape(D, T1).T
    return out


def kernel(**inputs):
    in_maps = prepare_in_maps(**inputs)
    nc = build_nc()
    res = run_bass_kernel_spmd(nc, in_maps, core_ids=list(range(8)))
    return assemble_out([res.results[c]["OUT"] for c in range(8)])


# version 16
# speedup vs baseline: 1.0449x; 1.0449x over previous
import numpy as np
import concourse.bass as bass
import concourse.mybir as mybir
from concourse.bass_utils import run_bass_kernel_spmd

F32 = mybir.dt.float32
BF16 = mybir.dt.bfloat16
U32 = mybir.dt.uint32
ACT = mybir.ActivationFunctionType
ALU = mybir.AluOpType

D = 4096
NCH = 32
T1 = 1024
NTB = 2
LMAP = [0, 2, 1, 3]
Q_START, Q_END_EXCL = 611, 4095
RCOUNT = float(Q_END_EXCL - Q_START)
EPS_LN = 1e-5
DBG = False


def ts(i, w):
    return slice(i * w, (i + 1) * w)


class Plan:
    ENGS = ("sync", "gpsimd", "vector", "scalar", "tensor")

    def __init__(self):
        self.q = {e: [] for e in self.ENGS}
        self.cnt = {}

    def bump(self, s, k):
        self.cnt[s] = self.cnt.get(s, 0) + k
        return self.cnt[s]

    def val(self, s):
        return self.cnt.get(s, 0)

    def emit(self, eng, fn):
        self.q[eng].append(fn)


def build_nc():
    nc = bass.Bass("TRN2", target_bir_lowering=False, detect_race_conditions=False)

    X = nc.dram_tensor("X", [2, 128, NCH, T1], F32, kind="ExternalInput")
    WK = nc.dram_tensor("WK", [2, 128, NCH, 512], F32, kind="ExternalInput")
    WV = nc.dram_tensor("WV", [2, 128, NCH, 512], F32, kind="ExternalInput")
    WQ = nc.dram_tensor("WQ", [16, 128, NCH, 256], F32, kind="ExternalInput")
    WO = nc.dram_tensor("WO", [16, 128, NCH, 256], F32, kind="ExternalInput")
    WFD = nc.dram_tensor("WFD", [16, 128, NCH, 256], F32, kind="ExternalInput")
    ACBD = nc.dram_tensor("ACBD", [128, NCH * 128], F32, kind="ExternalInput")
    BCBD = nc.dram_tensor("BCBD", [128, 4096], F32, kind="ExternalInput")
    RWTD = nc.dram_tensor("RWTD", [128, NCH * 8], F32, kind="ExternalInput")
    BQD = nc.dram_tensor("BQD", [128, 32], F32, kind="ExternalInput")
    BOD = nc.dram_tensor("BOD", [128, 32], F32, kind="ExternalInput")
    BFD = nc.dram_tensor("BFD", [128, 32], F32, kind="ExternalInput")
    BKR = nc.dram_tensor("BKR", [1, 1024], F32, kind="ExternalInput")
    BVR = nc.dram_tensor("BVR", [1, 1024], F32, kind="ExternalInput")
    RBR = nc.dram_tensor("RBR", [1, 8], F32, kind="ExternalInput")
    MSK = nc.dram_tensor("MSK", [2, 1, T1], F32, kind="ExternalInput")
    EMKD = nc.dram_tensor("EMKD", [128, 8], F32, kind="ExternalInput")
    SLT = nc.dram_tensor("SLT", [1, 8], U32, kind="ExternalInput")
    OUT = nc.dram_tensor("OUT", [NCH, 128, T1], F32, kind="ExternalOutput")

    XNSP = nc.dram_tensor("XNSP", [2, 128, NCH * T1], BF16, kind="Internal")
    RWS = nc.dram_tensor("RWS", [8, 1024], F32, kind="Internal")
    SH1D = nc.dram_tensor("SH1D", [2, 128, 1032], F32, kind="Internal", addr_space="Shared")
    SH1F = nc.dram_tensor("SH1F", [2, 1, 4], U32, kind="Internal", addr_space="Shared")
    SH2D = nc.dram_tensor("SH2D", [2, 128, 32], F32, kind="Internal", addr_space="Shared")
    SH2F = nc.dram_tensor("SH2F", [2, 1, 4], U32, kind="Internal", addr_space="Shared")

    BUF1 = nc.alloc_sbuf_tensor("BUF1", [128, NCH * T1], BF16)
    BUF2 = nc.alloc_sbuf_tensor("BUF2", [128, NCH * T1], BF16)
    PK = BUF2[:, 0:8192]
    PV = BUF2[:, 8192:16384]
    WR = nc.alloc_sbuf_tensor("WR", [128, 16384], BF16)
    TRI = nc.alloc_sbuf_tensor("TRI", [128, 4096], BF16)
    BCBS = nc.alloc_sbuf_tensor("BCBS", [128, 4096], BF16)
    EXG = nc.alloc_sbuf_tensor("EXG", [128, 3072], F32)
    KVBF = nc.alloc_sbuf_tensor("KVBF", [128, 1024], BF16)
    E0F = nc.alloc_sbuf_tensor("E0F", [128, 128], F32)
    E0B = nc.alloc_sbuf_tensor("E0B", [128, 128], BF16)
    ONESC = nc.alloc_sbuf_tensor("ONESC", [128, 1], BF16)
    PHIQ = nc.alloc_sbuf_tensor("PHIQ", [128, 1024], BF16)
    RKQ = nc.alloc_sbuf_tensor("RKQ", [128, 200], F32)
    QMB = nc.alloc_sbuf_tensor("QMB", [128, 32], BF16)
    BQT = nc.alloc_sbuf_tensor("BQT", [128, 96], F32)
    EMKS = nc.alloc_sbuf_tensor("EMKS", [128, 8], F32)
    SCL = nc.alloc_sbuf_tensor("SCL", [128, 1], F32)
    SLTS = nc.alloc_sbuf_tensor("SLTS", [1, 8], U32)
    FLS = nc.alloc_sbuf_tensor("FLS", [1, 4], U32)
    EPSS = nc.alloc_sbuf_tensor("EPSS", [1, 2], F32)

    M0, M1 = EXG[0:1, 0:512], EXG[0:1, 512:1024]
    RV0, RV1 = EXG[0:1, 1024:1536], EXG[0:1, 1536:2048]
    W0, W1 = EXG[0:1, 2048:2560], EXG[0:1, 2560:3072]
    MROW, RVROW = [M0, M1], [RV0, RV1]
    MVIEW = [EXG[:, 0:512], EXG[:, 512:1024]]
    RVVIEW = [EXG[:, 1024:1536], EXG[:, 1536:2048]]
    W0VIEW, W1VIEW = EXG[:, 2048:2560], EXG[:, 2560:3072]
    MT256, ET256 = EXG[:, 2048:2304], EXG[:, 2304:2560]
    MT512, ET512 = EXG[:, 2048:2560], EXG[:, 2560:3072]

    PS = [nc.alloc_psum_tensor(f"PS{i}", [128, 512], F32) for i in range(3)]
    PS3 = nc.alloc_psum_tensor("PS3", [128, 512], F32)
    PSKV = nc.alloc_psum_tensor("PSKV", [128, 1024], F32)
    PSKS = nc.alloc_psum_tensor("PSKS", [128, 512], F32)
    PS.append(nc.alloc_psum_tensor("PS4", [128, 512], F32))

    sems = ("GW", "SY", "WFR", "MM", "EVV", "EVS", "VS", "SV", "EX", "PX1", "PX2")
    sem = {n: nc.alloc_semaphore(n) for n in sems}

    p = Plan()
    bank_guard = {}

    def gdma(dst, src, pre=()):
        tgt = p.bump("GW", 16)

        def fn(e, dst=dst, src=src, pre=tuple(pre)):
            for s_, v_ in pre:
                e.wait_ge(sem[s_], v_)
            e.dma_start(out=dst, in_=src).then_inc(sem["GW"], 16)
        p.emit("gpsimd", fn)
        return tgt

    def sdma(dst, src, pre=()):
        tgt = p.bump("SY", 16)

        def fn(e, dst=dst, src=src, pre=tuple(pre)):
            for s_, v_ in pre:
                e.wait_ge(sem[s_], v_)
            e.dma_start(out=dst, in_=src).then_inc(sem["SY"], 16)
        p.emit("sync", fn)
        return tgt

    def ewait(eng, s, v):
        p.emit(eng, lambda e, s=s, v=v: e.wait_ge(sem[s], v))

    def einc(eng, s, k=1):
        tgt = p.bump(s, k)
        p.emit(eng, lambda e, s=s, k=k: e.nop().then_inc(sem[s], k))
        return tgt

    def t_mm(ps_ap, lhsT, rhs, start, stop, waits=(), inc=None):
        if inc is not None:
            p.bump(inc[0], inc[1])

        def fn(e, ps_ap=ps_ap, lhsT=lhsT, rhs=rhs, start=start, stop=stop,
               waits=tuple(waits), inc=inc):
            for s_, v_ in waits:
                e.wait_ge(sem[s_], v_)
            ins = e.matmul(ps_ap, lhsT, rhs, start=start, stop=stop)
            if inc is not None:
                ins.then_inc(sem[inc[0]], inc[1])
        p.emit("tensor", fn)

    def t_guard(bn):
        g = bank_guard.pop(bn, None)
        return [g] if g else []

    def v_op(fb, inc=None, waits=()):
        tgt = p.bump(inc, 1) if inc else None

        def fn(e, fb=fb, waits=tuple(waits), inc=inc):
            for s_, v_ in waits:
                e.wait_ge(sem[s_], v_)
            ins = fb(e)
            if inc:
                last = ins[-1] if isinstance(ins, (list, tuple)) else ins
                last.then_inc(sem[inc], 1)
        p.emit("vector", fn)
        return tgt

    def s_op(fb, inc=None, waits=()):
        tgt = p.bump(inc, 1) if inc else None

        def fn(e, fb=fb, waits=tuple(waits), inc=inc):
            for s_, v_ in waits:
                e.wait_ge(sem[s_], v_)
            ins = fb(e)
            if inc:
                last = ins[-1] if isinstance(ins, (list, tuple)) else ins
                last.then_inc(sem[inc], 1)
        p.emit("scalar", fn)
        return tgt

    ring = {"n": 0}

    def ring_load(src_ap, width=8192, pre=()):
        n = ring["n"]
        slot = WR[:, n % 2 * 8192: n % 2 * 8192 + width]
        waits = list(pre)
        if n >= 2:
            waits.append(("WFR", n - 1))
        tgt = gdma(slot, src_ap, pre=waits)
        ring["n"] = n + 1
        return slot, tgt

    def ring_free():
        return einc("tensor", "WFR")

    rot = {"i": 0}

    rot_names = ["PS0", "PS1", "PS2", "PS4"]

    def next_bank():
        i = rot["i"] % 4
        rot["i"] += 1
        return rot_names[i], PS[i]

    def bsl(buf, c, tb):
        return buf[:, c * T1 + tb * 512: c * T1 + (tb + 1) * 512]

    def ln_stats(src_of, sq_slots, acc, keys, pers_rows=None, src_ready=None):
        res = []
        prev_dma = None
        for tb in range(NTB):
            sum_ps, sq_ps = acc[tb]
            kb_sum, kb_sq = keys[tb]
            gsum, gsq = t_guard(kb_sum), t_guard(kb_sq)
            mm_sq = {}
            for c in range(NCH):
                src = src_of(c, tb)
                slot = sq_slots[c % 2]
                w = [("MM", mm_sq[c - 2])] if c - 2 in mm_sq else []
                if c == 0 and src_ready is not None:
                    w = w + [("VS", src_ready)]
                sq_t = s_op(lambda e, s_=slot, a=src: e.activation(
                    s_, a, ACT.Square), inc="SV", waits=w)
                sw0 = (gsum if c == 0 else [])
                if c == 0 and src_ready is not None:
                    sw0 = sw0 + [("VS", src_ready)]
                t_mm(sum_ps, ONESC[:], src, start=(c == 0), stop=(c == NCH - 1),
                     waits=sw0)
                t_mm(sq_ps, ONESC[:], slot, start=(c == 0), stop=(c == NCH - 1),
                     waits=[("SV", sq_t)] + (gsq if c == 0 else []), inc=("MM", 1))
                mm_sq[c] = p.val("MM")
            mm_done = p.val("MM")

            def fin(e, tb=tb, sum_ps=sum_ps, sq_ps=sq_ps):
                e.tensor_scalar(MROW[tb], sum_ps, 1.0 / D, None, ALU.mult)
                e.tensor_mul(W0, MROW[tb], MROW[tb])
                return e.scalar_tensor_tensor(W1, sq_ps, 1.0 / D, W0,
                                              ALU.mult, ALU.subtract)
            v_f = v_op(fin, inc="VS", waits=[("MM", mm_done), ("SV", p.val("SV"))] +
                       ([("SY", prev_dma)] if prev_dma else []))
            bank_guard[kb_sum] = ("VS", v_f)
            bank_guard[kb_sq] = ("VS", v_f)
            s_q = s_op(lambda e: e.activation(W1, W1, ACT.Sqrt, bias=EPSS[0:1, 0:1]),
                       inc="SV", waits=[("VS", v_f)])
            v_r = v_op(lambda e, tb=tb: e.reciprocal(RVROW[tb], W1),
                       inc="VS", waits=[("SV", s_q)])
            if pers_rows is not None:
                m_i, s_i = pers_rows
                sdma(RWS[m_i, ts(tb, 512)], MROW[tb], pre=[("VS", v_f)])
                prev_dma = sdma(RWS[s_i, ts(tb, 512)], W1, pre=[("VS", v_r)])
            res.append(v_r)
        return res


    v_c0 = v_op(lambda e: (
        e.memset(E0F[:], 0.0), e.memset(E0B[:], 0.0),
        e.memset(ONESC[:], 1.0), e.memset(EPSS[:], EPS_LN),
        e.memset(EXG[:], 0.0), e.memset(TRI[:], 0.0))[-1], inc="VS")
    v_consts = v_op(lambda e: (
        e.memset(E0F[0:1, :], 1.0), e.memset(E0B[0:1, :], 1.0))[-1],
        inc="VS", waits=[("VS", v_c0)])

    ld_slt = sdma(SLTS[:], SLT[:])
    sdma(BQT[:, 0:32], BQD[:])
    sdma(BQT[:, 32:64], BOD[:])
    ld_bqt = sdma(BQT[:, 64:96], BFD[:])
    ld_emk = sdma(EMKS[:], EMKD[:])

    ld_bk = gdma(TRI[0:1, 0:1024], BKR[:], pre=[("VS", v_consts)])
    ld_bv = gdma(TRI[0:1, 1024:2048], BVR[:])
    ld_bcb = gdma(BCBS[:], BCBD[:])

    kv_mm_done = [0, 0]
    spill_done = [0, 0]

    for s in (0, 1):
        pre = [] if s == 0 else [("SY", spill_done[0]), ("MM", kv_mm_done[0])]
        gdma(BUF1[:, 0:16384], X[s][:, 0:16, :], pre=pre)
        x_tgt = gdma(BUF1[:, 16384:32768], X[s][:, 16:32, :])

        ewait("tensor", "GW", x_tgt)
        ewait("vector", "GW", x_tgt)
        ewait("scalar", "GW", x_tgt)
        if s == 0:
            ewait("tensor", "VS", v_consts)
        ln_stats(lambda c, tb: bsl(BUF1, c, tb),
                 (TRI[:, 2048:2560], TRI[:, 2560:3072]),
                 acc=[(PS[0][0:1, 0:512], PS[1][0:1, 0:512])] * 2,
                 keys=[("PS0", "PS1")] * 2,
                 pers_rows=(2 * s + 0, 2 * s + 1))

        for tb in range(NTB):
            g2 = t_guard("PS2")
            t_mm(PS[2][:, 0:512], E0F[:], MVIEW[tb], start=True, stop=True,
                 waits=g2 + [("VS", p.val("VS"))])
            g0 = t_guard("PS0")
            t_mm(PS[0][:, 0:512], E0F[:], RVVIEW[tb], start=True, stop=True,
                 waits=g0, inc=("MM", 1))
            bc_done = p.val("MM")
            v_cp = v_op(lambda e: (
                e.tensor_copy(TRI[:, 3072:3584], PS[2][:, 0:512]),
                e.tensor_copy(TRI[:, 3584:4096], PS[0][:, 0:512]))[-1],
                inc="VS", waits=[("MM", bc_done)])
            bank_guard["PS2"] = ("VS", v_cp)
            bank_guard["PS0"] = ("VS", v_cp)
            for c in range(NCH):
                sl = bsl(BUF1, c, tb)
                v_op(lambda e, sl=sl: (
                    e.tensor_sub(sl, sl, TRI[:, 3072:3584]),
                    e.tensor_mul(sl, sl, TRI[:, 3584:4096]))[-1],
                    inc="VS", waits=([("VS", v_cp)] if c == 0 else []))
        v_norm = p.val("VS")

        sdma(XNSP[s][:, 0:16384], BUF1[:, 0:16384], pre=[("VS", v_norm)])
        spill_done[s] = sdma(XNSP[s][:, 16384:32768], BUF1[:, 16384:32768])

        KVW = [WR[:, 0:16384], BUF2[:, 16384:32768]]
        for wb in range(4):
            src = WK[wb] if wb < 2 else WV[wb - 2]
            n = ring["n"]
            slot = KVW[n % 2]
            wwaits = [("WFR", n - 1)] if n >= 2 else []
            wtgt = gdma(slot, src, pre=wwaits)
            ring["n"] = n + 1
            carrier = TRI[:, (wb % 2) * 512:(wb % 2) * 512 + 512] if wb < 2 else \
                TRI[:, 1024 + (wb % 2) * 512:1024 + (wb % 2) * 512 + 512]
            ewait("tensor", "GW", wtgt)
            if wb == 0:
                ewait("tensor", "GW", max(ld_bk, ld_bv))
                ewait("tensor", "VS", v_norm)
            for tile in range(8):
                bn, bap = next_bank()
                pap = bap[:, 0:512]
                g = t_guard(bn)
                for c in range(NCH):
                    t_mm(pap, BUF1[:, c * T1 + tile * 128: c * T1 + tile * 128 + 128],
                         slot[:, ts(c, 512)], start=(c == 0), stop=False,
                         waits=(g if c == 0 else []))
                t_mm(pap, E0B[:], carrier, start=False, stop=True, inc=("MM", 1))
                mm_t = p.val("MM")
                dc = tile * 1024 + (wb % 2) * 512
                extra = [("MM", kv_mm_done[0])] if (s == 1 and wb == 0 and tile == 0) else []
                if wb < 2:
                    vm = v_op(lambda e, pap=pap: e.tensor_single_scalar(
                        MT512, pap, 0.0, ALU.min), inc="VS",
                        waits=[("MM", mm_t), ("SV", p.val("SV"))] + extra)
                    se = s_op(lambda e: e.activation(ET512, MT512, ACT.Exp),
                              inc="SV", waits=[("VS", vm), ("EVV", p.val("EVV"))])
                    vp = v_op(lambda e, pap=pap, dc=dc: e.scalar_tensor_tensor(
                        PK[:, dc:dc + 512], pap, 0.0, ET512, ALU.max, ALU.add),
                        inc="EVV", waits=[("SV", se)])
                else:
                    vp = v_op(lambda e, pap=pap, dc=dc: e.tensor_copy(
                        PV[:, dc:dc + 512], pap), inc="EVV",
                        waits=[("MM", mm_t)] + extra)
                bank_guard[bn] = ("EVV", vp)
            ring_free()
        kv_evv = p.val("EVV")

        ewait("tensor", "EVV", kv_evv)
        for tile in range(8):
            for h in range(8):
                kb = PK[:, tile * 1024 + h * 128: tile * 1024 + (h + 1) * 128]
                vb = PV[:, tile * 1024 + h * 128: tile * 1024 + (h + 1) * 128]
                t_mm(PSKV[:, ts(h, 128)], kb, vb,
                     start=(s == 0 and tile == 0 and h in (0, 4)),
                     stop=(s == 1 and tile == 7 and h in (3, 7)))
                t_mm(PSKS[:, h:h + 1], kb, ONESC[:],
                     start=(s == 0 and tile == 0 and h == 0),
                     stop=(s == 1 and tile == 7 and h == 7),
                     inc=(("MM", 1) if (tile == 7 and h == 7) else None))
        kv_mm_done[s] = p.val("MM")

    exg_my = EXG[:, 0:1032]
    exg_oth = EXG[:, 1040:2072]
    v_kvst = v_op(lambda e: (
        e.tensor_copy(exg_my[:, 0:1024], PSKV[:]),
        e.tensor_copy(exg_my[:, 1024:1032], PSKS[:, 0:8]))[-1],
        inc="EVV", waits=[("MM", kv_mm_done[1])])
    bank_guard["KVA"] = ("EVV", v_kvst)
    bank_guard["KVB"] = ("EVV", v_kvst)
    bank_guard["KSA"] = ("EVV", v_kvst)

    def emit_exchange(D_, F_, my_pay, oth_pay, nonce_word, pxname, pre=()):
        def fn(e, pre=tuple(pre)):
            for s_, v_ in pre:
                e.wait_ge(sem[s_], v_)
            rmy = e.to_reg(e.value_load(SLTS[0:1, 0:1]))
            rnc = e.to_reg(e.value_load(SLTS[0:1, nonce_word:nonce_word + 1]))

            def one(my, oth):
                tag = f"{pxname}_{my}"
                e.dma_start(out=D_[my], in_=my_pay).then_inc(sem[pxname], 16)
                e.wait_ge(sem[pxname], 16)
                e.dma_start(out=F_[my], in_=SLTS[0:1, 0:4]).then_inc(sem[pxname], 16)
                with e.register("rc_" + tag) as rc, e.register("rn_" + tag) as rn:
                    e.reg_mov(rc, 1)
                    e.reg_mov(rn, 32)
                    with e.While(rc):
                        e.dma_start(out=FLS[:], in_=F_[oth]).then_inc(sem[pxname], 16)
                        e.reg_add(rn, rn, 16)
                        e.wait_ge(sem[pxname], rn)
                        rf = e.value_load(FLS[0:1, nonce_word:nonce_word + 1])
                        e.reg_alu(rc, rf, rnc, ALU.not_equal)
                    e.dma_start(out=oth_pay, in_=D_[oth]).then_inc(sem[pxname], 16)
                    e.reg_add(rn, rn, 16)
                    e.wait_ge(sem[pxname], rn)
            with e.If_eq(rmy, 0):
                one(0, 1)
            with e.Else():
                one(1, 0)
            e.nop().then_inc(sem["EX"], 1)
        p.emit("sync", fn)
        return p.bump("EX", 1)

    ex1 = emit_exchange(SH1D, SH1F, exg_my, exg_oth, 1, "PX1",
                        pre=[("EVV", v_kvst), ("SY", ld_slt)])

    def kv_combine(e):
        e.tensor_add(KVBF[:], exg_my[:, 0:1024], exg_oth[:, 0:1024])
        e.tensor_add(RKQ[:, 0:8], exg_my[:, 1024:1032], exg_oth[:, 1024:1032])
        e.tensor_scalar(RKQ[:, 0:8], RKQ[:, 0:8], 1e-10, None, ALU.add)
        ins = e.reciprocal(RKQ[:, 0:8], RKQ[:, 0:8])
        for _ in range(6):
            ins = e.tensor_copy(TRI[:, 3584:4096], BCBS[:, 0:512])
        return ins
    v_kv_ready = v_op(kv_combine, inc="VS", waits=[("EX", ex1)])

    router_done = {}
    rering_guard = {k: ("MM", kv_mm_done[1]) for k in range(4)}

    def phase_b(s, gate_vs=None, gate_evs_buf2=None):
        if s == 0:
            b1 = sdma(BUF1[:, 0:16384], XNSP[0][:, 0:16384],
                      pre=[("VS", gate_vs[0])])
            b2 = sdma(BUF1[:, 16384:32768], XNSP[0][:, 16384:32768],
                      pre=[("VS", gate_vs[1])])
            ewait("tensor", "SY", b2)

        phiq_free = p.val("MM")
        for om in range(16):
            slot, wtgt = ring_load(WQ[om])
            ewait("tensor", "GW", wtgt)
            for m in range(2):
                h = om * 2 + m
                kvh = h // 4
                banks = [next_bank(), next_bank()]
                gq = t_guard(banks[0][0]) + t_guard(banks[1][0])
                for c in range(NCH):
                    lhs = slot[:, c * 256 + m * 128: c * 256 + (m + 1) * 128]
                    for tb in range(NTB):
                        t_mm(banks[tb][1][:, 0:512], lhs, bsl(BUF1, c, tb),
                             start=(c == 0), stop=(c == NCH - 1),
                             waits=(gq if (c == 0 and tb == 0) else []),
                             inc=(("MM", 1) if (c == NCH - 1 and tb == 1) else None))
                mm_q = p.val("MM")
                for tb in range(NTB):
                    bn, bap = banks[tb]
                    vm = v_op(lambda e, bap=bap, h=h: e.tensor_scalar(
                        MT512, bap[:, 0:512], BQT[:, h:h + 1], 0.0, ALU.add, ALU.min),
                        inc="VS", waits=[("MM", mm_q), ("MM", phiq_free),
                                         ("SV", p.val("SV"))])
                    se = s_op(lambda e: e.activation(ET512, MT512, ACT.Exp),
                              inc="SV", waits=[("VS", vm), ("EVV", p.val("EVV"))])
                    sr = s_op(lambda e, bap=bap, h=h: e.activation(
                        bap[:, 0:512], bap[:, 0:512], ACT.Relu,
                        bias=BQT[:, h:h + 1]), inc="SV")
                    vp = v_op(lambda e, bap=bap, tb=tb: e.tensor_add(
                        PHIQ[:, ts(tb, 512)], bap[:, 0:512], ET512),
                        inc="EVV", waits=[("SV", sr)])
                    bank_guard[bn] = ("EVV", vp)
                for tb in range(NTB):
                    bn2, bap2 = next_bank()
                    g2 = t_guard(bn2)
                    t_mm(bap2[:, 0:512], KVBF[:, ts(kvh, 128)], PHIQ[:, ts(tb, 512)],
                         start=True, stop=True,
                         waits=g2 + [("EVV", p.val("EVV")), ("VS", v_kv_ready)],
                         inc=("MM", 1))
                    mm_a = p.val("MM")
                    ew = [("MM", mm_a)]
                    if gate_evs_buf2 is not None and h == 0 and tb == 0:
                        ew.append(("VS", gate_evs_buf2))
                    sa = v_op(lambda e, bap2=bap2, h=h, tb=tb, kvh=kvh: e.tensor_scalar(
                        bsl(BUF2, h, tb), bap2[:, 0:512], RKQ[:, kvh:kvh + 1], None,
                        ALU.mult),
                        inc="EVV", waits=ew)
                    bank_guard[bn2] = ("EVV", sa)
                phiq_free = p.val("MM")
            ring_free()

        ewait("tensor", "EVV", p.val("EVV"))
        ewait("tensor", "EVS", p.val("EVS"))
        sq_mm = {}
        for om in range(16):
            slot, wtgt = ring_load(WO[om])
            ewait("tensor", "GW", wtgt)
            for m in range(2):
                oc = om * 2 + m
                banks = [next_bank(), next_bank()]
                gq = t_guard(banks[0][0]) + t_guard(banks[1][0])
                for c in range(NCH):
                    lhs = slot[:, c * 256 + m * 128: c * 256 + (m + 1) * 128]
                    for tb in range(NTB):
                        t_mm(banks[tb][1][:, 0:512], lhs, bsl(BUF2, c, tb),
                             start=(c == 0), stop=(c == NCH - 1),
                             waits=(gq if (c == 0 and tb == 0) else []),
                             inc=(("MM", 1) if (c == NCH - 1 and tb == 1) else None))
                mm_o = p.val("MM")
                for tb in range(NTB):
                    bn, bap = banks[tb]
                    sqsl = TRI[:, 2048 + ((2 * oc + tb) % 2) * 512:
                               2560 + ((2 * oc + tb) % 2) * 512]
                    sw = [("MM", mm_o)]
                    if (oc - 1, tb) in sq_mm:
                        sw.append(("MM", sq_mm[(oc - 1, tb)]))
                    so = s_op(lambda e, bap=bap, oc=oc, tb=tb, sqsl=sqsl: (
                        e.activation(bsl(BUF1, oc, tb), bap[:, 0:512], ACT.Identity,
                                     bias=BQT[:, 32 + oc:33 + oc]),
                        e.activation(sqsl, bap[:, 0:512], ACT.Square,
                                     bias=BQT[:, 32 + oc:33 + oc]))[-1],
                        inc="EVS", waits=sw)
                    bank_guard[bn] = ("EVS", so)
                    sum_ps = PSKV[0:1, ts(tb, 512)]
                    sq_ps = PSKS[0:1, 0:512] if tb == 0 else PS3[0:1, 0:512]
                    kb_sum = "KVA" if tb == 0 else "KVB"
                    kb_sq = "KSA" if tb == 0 else "PS3"
                    t_mm(sum_ps, ONESC[:], bsl(BUF1, oc, tb),
                         start=(oc == 0), stop=(oc == 31),
                         waits=[("EVS", so)] + (t_guard(kb_sum) if oc == 0 else []))
                    t_mm(sq_ps, ONESC[:], sqsl, start=(oc == 0), stop=(oc == 31),
                         waits=(t_guard(kb_sq) if oc == 0 else []), inc=("MM", 1))
                    sq_mm[(oc, tb)] = p.val("MM")
            ring_free()
        mm_ost = p.val("MM")

        fin = []
        for tb in range(NTB):
            sum_ps = PSKV[0:1, ts(tb, 512)]
            sq_ps = PSKS[0:1, 0:512] if tb == 0 else PS3[0:1, 0:512]

            def fo(e, tb=tb, sum_ps=sum_ps, sq_ps=sq_ps):
                e.tensor_scalar(MROW[tb], sum_ps, 1.0 / D, None, ALU.mult)
                e.tensor_mul(W0, MROW[tb], MROW[tb])
                return e.scalar_tensor_tensor(W1, sq_ps, 1.0 / D, W0,
                                              ALU.mult, ALU.subtract)
            v_f = v_op(fo, inc="VS", waits=[("MM", mm_ost), ("EX", ex1),
                                            ("SV", p.val("SV"))])
            s_q = s_op(lambda e: e.activation(W1, W1, ACT.Sqrt, bias=EPSS[0:1, 0:1]),
                       inc="SV", waits=[("VS", v_f)])
            v_r = v_op(lambda e, tb=tb: e.reciprocal(RVROW[tb], W1),
                       inc="VS", waits=[("SV", s_q)])
            ldm = sdma(W0, RWS[2 * s + 0, ts(tb, 512)], pre=[("VS", v_r)])
            v_cc = v_op(lambda e, tb=tb: (
                e.tensor_mul(MROW[tb], MROW[tb], RVROW[tb]),
                e.scalar_tensor_tensor(MROW[tb], MROW[tb], -1.0, W0,
                                       ALU.mult, ALU.add))[-1],
                inc="VS", waits=[("SY", ldm)])
            fin.append(v_cc)
        bank_guard["KVA"] = ("VS", fin[0])
        bank_guard["KSA"] = ("VS", fin[0])
        bank_guard["KVB"] = ("VS", fin[1])
        bank_guard["PS3"] = ("VS", fin[1])

        rering = [TRI[:, i * 512:(i + 1) * 512] for i in range(4)]
        bcA, bcB, bcC = TRI[:, 2048:2560], TRI[:, 2560:3072], TRI[:, 3072:3584]
        TMPB = TRI[:, 3584:4096]
        v_x1_last = 0
        v_x1_half = 0
        prev_bc_mm = [0]
        for tb in range(NTB):
            ldsa = sdma(W1, RWS[2 * s + 1, ts(tb, 512)],
                        pre=[("VS", fin[tb]), ("MM", prev_bc_mm[0])])
            bn, bap = next_bank()
            t_mm(bap[:, 0:512], E0F[:], RVVIEW[tb], start=True, stop=True,
                 waits=t_guard(bn) + [("VS", fin[tb])])
            bn2, bap2 = next_bank()
            t_mm(bap2[:, 0:512], E0F[:], MVIEW[tb], start=True, stop=True,
                 waits=t_guard(bn2))
            bn3, bap3 = next_bank()
            t_mm(bap3[:, 0:512], E0F[:], W1VIEW, start=True, stop=True,
                 waits=t_guard(bn3) + [("SY", ldsa)], inc=("MM", 1))
            mm_bc = p.val("MM")
            prev_bc_mm[0] = mm_bc
            v_cp = v_op(lambda e, bap=bap, bap2=bap2, bap3=bap3: (
                e.tensor_copy(bcA, bap[:, 0:512]),
                e.tensor_copy(bcB, bap2[:, 0:512]),
                e.tensor_copy(bcC, bap3[:, 0:512]))[-1],
                inc="VS", waits=[("MM", mm_bc)])
            bank_guard[bn] = ("VS", v_cp)
            bank_guard[bn2] = ("VS", v_cp)
            bank_guard[bn3] = ("VS", v_cp)
            for c in range(NCH):
                key = c % 4
                prew = [rering_guard[key]] if rering_guard[key] else []
                ldx = sdma(rering[key],
                           XNSP[s][:, c * T1 + tb * 512: c * T1 + (tb + 1) * 512],
                           pre=prew)
                vv = v_op(lambda e, c=c, tb=tb, key=key: (
                    e.tensor_mul(TMPB, bsl(BUF1, c, tb), bcA),
                    e.tensor_add(TMPB, TMPB, bcB),
                    e.tensor_mul(rering[key], rering[key], bcC),
                    e.tensor_add(bsl(BUF2, c, tb), TMPB, rering[key]))[-1],
                    inc="VS",
                    waits=[("VS", v_cp), ("SY", ldx)] if c == 0 else [("SY", ldx)])
                rering_guard[key] = ("VS", vv)
                if tb == 1 and c == 15:
                    v_x1_half = p.val("VS")
            v_x1_last = p.val("VS")

        ln_stats(lambda c, tb: bsl(BUF2, c, tb),
                 (TRI[:, 2048:2560], TRI[:, 2560:3072]),
                 acc=[(PSKV[0:1, 0:512], PSKS[0:1, 0:512]),
                      (PSKV[0:1, 512:1024], PS3[0:1, 0:512])],
                 keys=[("KVA", "KSA"), ("KVB", "PS3")],
                 pers_rows=((4, 5) if s == 0 else None),
                 src_ready=v_x1_last)

        rs_col = RKQ[:, 8:40] if s == 1 else RKQ[:, 40:72]
        rs2_col = RKQ[:, 168:200]
        for tb in range(NTB):
            ldmsk = sdma(W1[0:1, 0:512], MSK[s][:, ts(tb, 512)],
                         pre=[("VS", p.val("VS"))])
            bn, bap = next_bank()
            t_mm(bap[:, 0:512], E0F[:], MVIEW[tb], start=True, stop=True,
                 waits=t_guard(bn) + [("VS", p.val("VS"))])
            bn2, bap2 = next_bank()
            t_mm(bap2[:, 0:512], E0F[:], RVVIEW[tb], start=True, stop=True,
                 waits=t_guard(bn2))
            bn3, bap3 = next_bank()
            t_mm(bap3[:, 0:512], E0F[:], W1VIEW, start=True, stop=True,
                 waits=t_guard(bn3) + [("SY", ldmsk)], inc=("MM", 1))
            mm_bc = p.val("MM")
            v_cp = v_op(lambda e, bap=bap, bap2=bap2, bap3=bap3: (
                e.tensor_copy(bcA, bap[:, 0:512]),
                e.tensor_copy(bcB, bap2[:, 0:512]),
                e.tensor_copy(bcC, bap3[:, 0:512]))[-1],
                inc="VS", waits=[("MM", mm_bc)])
            bank_guard[bn] = ("VS", v_cp)
            bank_guard[bn2] = ("VS", v_cp)
            bank_guard[bn3] = ("VS", v_cp)
            for c in range(NCH):
                sl = bsl(BUF2, c, tb)
                v_op(lambda e, sl=sl: (
                    e.tensor_sub(sl, sl, bcA),
                    e.tensor_mul(sl, sl, bcB))[-1],
                    inc="VS", waits=([("VS", v_cp)] if c == 0 else []))
                v_op(lambda e, sl=sl: e.tensor_mul(TMPB, sl, bcC), inc="VS")
                dst = rs_col[:, c:c + 1] if tb == 0 else rs2_col[:, c:c + 1]
                v_op(lambda e, dst=dst: e.reduce_sum(
                    dst, TMPB, mybir.AxisListType.X), inc="VS")
        v_op(lambda e, rs_col=rs_col: e.tensor_add(rs_col, rs_col, rs2_col),
             inc="VS")
        router_done[s] = p.val("VS")
        return v_x1_half, v_x1_last

    v_x1_s1 = phase_b(1)
    v_x1_s0 = phase_b(0, gate_vs=v_x1_s1, gate_evs_buf2=router_done[1])

    v_pay = v_op(lambda e: e.tensor_add(RKQ[:, 72:104], RKQ[:, 8:40], RKQ[:, 40:72]),
                 inc="VS", waits=[("VS", router_done[0])])
    ex2 = emit_exchange(SH2D, SH2F, RKQ[:, 72:104], RKQ[:, 104:136], 2, "PX2",
                        pre=[("EX", ex1), ("VS", v_pay)])
    def qm_fold(e):
        ins = e.tensor_add(RKQ[:, 8:40], RKQ[:, 72:104], RKQ[:, 104:136])
        for _ in range(6):
            ins = e.tensor_copy(TRI[:, 3584:4096], BCBS[:, 0:512])
        return ins
    v_qm = v_op(qm_fold, inc="VS", waits=[("EX", ex2)])

    rw_slot, rw_tgt = ring_load(RWTD[:], width=256)
    PROD = TRI[:, 0:256]
    v_prod = None
    for c in range(NCH):
        w = [("VS", v_qm), ("GW", rw_tgt)] if c == 0 else []
        v_prod = v_op(lambda e, c=c: e.tensor_scalar(
            PROD[:, ts(c, 8)], rw_slot[:, ts(c, 8)], RKQ[:, 8 + c:9 + c], None,
            ALU.mult), inc="VS", waits=w)
    g = t_guard("PS0")
    for c in range(NCH):
        t_mm(PS[0][0:1, 0:8], ONESC[:], PROD[:, ts(c, 8)],
             start=(c == 0), stop=(c == NCH - 1),
             waits=(g + [("VS", v_prod)] if c == 0 else []),
             inc=(("MM", 1) if c == NCH - 1 else None))
    mm_lg = p.val("MM")
    ring_free()
    ld_rb = sdma(W1[0:1, 0:8], RBR[:], pre=[])

    v_sm1 = v_op(lambda e: e.tensor_add(W0[0:1, 0:8], PS[0][0:1, 0:8],
                                        W1[0:1, 0:8]),
                 inc="VS", waits=[("MM", mm_lg), ("SY", ld_rb)])
    bank_guard["PS0"] = ("VS", v_sm1)
    if DBG:
        sdma(RWS[6, 0:256], EXG[0:1, 2304:2560], pre=[("VS", v_sm1)])
        sdma(RWS[7, 0:8], W0[0:1, 0:8])
    g = t_guard("PS1")
    t_mm(PS[1][:, 0:8], E0F[:], W0VIEW[:, 0:8], start=True, stop=True,
         waits=g + [("VS", v_sm1)], inc=("MM", 1))
    mm_rbc = p.val("MM")
    s_exp = s_op(lambda e: e.activation(RKQ[:, 144:152], PS[1][:, 0:8], ACT.Exp),
                 inc="SV", waits=[("MM", mm_rbc)])
    def scl_fold(e):
        def age(n=4):
            for _ in range(n):
                e.tensor_copy(TRI[:, 3584:4096], BCBS[:, 0:512])
        e.reduce_sum(RKQ[:, 152:153], RKQ[:, 144:152], mybir.AxisListType.X)
        age()
        e.reciprocal(RKQ[:, 153:154], RKQ[:, 152:153])
        e.tensor_mul(RKQ[:, 136:144], RKQ[:, 144:152], EMKS[:])
        age()
        e.reduce_sum(RKQ[:, 154:155], RKQ[:, 136:144], mybir.AxisListType.X)
        age()
        ins = e.tensor_mul(SCL[:], RKQ[:, 154:155], RKQ[:, 153:154])
        age(6)
        return e.tensor_copy(TRI[:, 3584:4096], BCBS[:, 0:512])
    v_scl = v_op(scl_fold, inc="VS", waits=[("SV", s_exp), ("SY", ld_emk)])
    bank_guard["PS1"] = ("VS", v_scl)

    ac_slot, ac_tgt = ring_load(ACBD[:], width=4096)
    ewait("tensor", "GW", ac_tgt)
    ewait("tensor", "VS", router_done[0])
    tsc = [TRI[:, 0:512], TRI[:, 512:1024]]
    for tb in range(NTB):
        bn, bap = next_bank()
        g = t_guard(bn)
        for c in range(NCH):
            t_mm(bap[:, 0:512], ac_slot[:, ts(c, 128)], bsl(BUF2, c, tb),
                 start=(c == 0), stop=(c == NCH - 1),
                 waits=(g if c == 0 else []), inc=(("MM", 1) if c == NCH - 1 else None))
        mm_t = p.val("MM")
        vt = v_op(lambda e, bap=bap, tb=tb: e.tensor_scalar(
            tsc[tb], bap[:, 0:512], SCL[:, 0:1], None, ALU.mult),
            inc="EVV", waits=[("MM", mm_t), ("VS", v_scl)])
        bank_guard[bn] = ("EVV", vt)
    ring_free()
    evv_t = p.val("EVV")

    ewait("tensor", "GW", ld_bcb)
    ewait("tensor", "EVV", evv_t)
    sq_mm = {}
    for om in range(16):
        slot, wtgt = ring_load(WFD[om])
        ewait("tensor", "GW", wtgt)
        for m in range(2):
            oc = om * 2 + m
            banks = [next_bank(), next_bank()]
            gq = t_guard(banks[0][0]) + t_guard(banks[1][0])
            for c in range(NCH):
                lhs = slot[:, c * 256 + m * 128: c * 256 + (m + 1) * 128]
                for tb in range(NTB):
                    t_mm(banks[tb][1][:, 0:512], lhs, bsl(BUF2, c, tb),
                         start=(c == 0), stop=False,
                         waits=(gq if (c == 0 and tb == 0) else []))
            for tb in range(NTB):
                t_mm(banks[tb][1][:, 0:512], BCBS[:, ts(oc, 128)], tsc[tb],
                     start=False, stop=True,
                     inc=(("MM", 1) if tb == 1 else None))
            mm_o = p.val("MM")
            for tb in range(NTB):
                bn, bap = banks[tb]
                sqsl = TRI[:, 2048 + ((2 * oc + tb) % 2) * 512:
                           2560 + ((2 * oc + tb) % 2) * 512]
                sw = [("MM", mm_o)]
                if (oc - 1, tb) in sq_mm:
                    sw.append(("MM", sq_mm[(oc - 1, tb)]))
                so = s_op(lambda e, bap=bap, oc=oc, tb=tb, sqsl=sqsl: (
                    e.activation(bsl(BUF1, oc, tb), bap[:, 0:512], ACT.Identity,
                                 bias=BQT[:, 64 + oc:65 + oc]),
                    e.activation(sqsl, bap[:, 0:512], ACT.Square,
                                 bias=BQT[:, 64 + oc:65 + oc]))[-1],
                    inc="EVS", waits=sw)
                bank_guard[bn] = ("EVS", so)
                sum_ps = PSKV[0:1, ts(tb, 512)]
                sq_ps = PSKS[0:1, 0:512] if tb == 0 else PS3[0:1, 0:512]
                kb_sum = "KVA" if tb == 0 else "KVB"
                kb_sq = "KSA" if tb == 0 else "PS3"
                t_mm(sum_ps, ONESC[:], bsl(BUF1, oc, tb),
                     start=(oc == 0), stop=(oc == 31),
                     waits=[("EVS", so)] + (t_guard(kb_sum) if oc == 0 else []))
                t_mm(sq_ps, ONESC[:], sqsl, start=(oc == 0), stop=(oc == 31),
                     waits=(t_guard(kb_sq) if oc == 0 else []), inc=("MM", 1))
                sq_mm[(oc, tb)] = p.val("MM")
        ring_free()
    mm_o2 = p.val("MM")

    fin2 = []
    for tb in range(NTB):
        sum_ps = PSKV[0:1, ts(tb, 512)]
        sq_ps = PSKS[0:1, 0:512] if tb == 0 else PS3[0:1, 0:512]

        def fo2(e, tb=tb, sum_ps=sum_ps, sq_ps=sq_ps):
            e.tensor_scalar(MROW[tb], sum_ps, 1.0 / D, None, ALU.mult)
            e.tensor_mul(W0, MROW[tb], MROW[tb])
            return e.scalar_tensor_tensor(W1, sq_ps, 1.0 / D, W0, ALU.mult, ALU.subtract)
        v_f = v_op(fo2, inc="VS", waits=[("MM", mm_o2), ("SV", p.val("SV"))])
        s_q = s_op(lambda e: e.activation(W1, W1, ACT.Sqrt, bias=EPSS[0:1, 0:1]),
                   inc="SV", waits=[("VS", v_f)])
        v_r = v_op(lambda e, tb=tb: e.reciprocal(RVROW[tb], W1), inc="VS",
                   waits=[("SV", s_q)])
        ldm1 = sdma(W0, RWS[4, ts(tb, 512)], pre=[("VS", v_r)])
        v_cc = v_op(lambda e, tb=tb: (
            e.tensor_mul(MROW[tb], MROW[tb], RVROW[tb]),
            e.scalar_tensor_tensor(MROW[tb], MROW[tb], -1.0, W0, ALU.mult, ALU.add))[-1],
            inc="VS", waits=[("SY", ldm1)])
        fin2.append(v_cc)
    bank_guard["KVA"] = ("VS", fin2[0])
    bank_guard["KSA"] = ("VS", fin2[0])
    bank_guard["KVB"] = ("VS", fin2[1])
    bank_guard["PS3"] = ("VS", fin2[1])

    bc_banks = {}
    bank_names = ["PS0", "PS1", "PS2", "KVA", "KVB", "KSA"]
    bank_aps = [PS[0][:, 0:512], PS[1][:, 0:512], PS[2][:, 0:512],
                PSKV[:, 0:512], PSKV[:, 512:1024], PSKS[:, 0:512]]
    ldx1 = [0, 0]
    prev_bcf = [0]
    for tb in range(NTB):
        lds1 = sdma(W1, RWS[5, ts(tb, 512)],
                    pre=[("VS", fin2[tb]), ("MM", prev_bcf[0])])
        ldx1[tb] = lds1
        i0 = tb * 3
        t_mm(bank_aps[i0], E0F[:], RVVIEW[tb], start=True, stop=True,
             waits=t_guard(bank_names[i0]) + [("VS", fin2[tb])])
        t_mm(bank_aps[i0 + 1], E0F[:], MVIEW[tb], start=True, stop=True,
             waits=t_guard(bank_names[i0 + 1]))
        t_mm(bank_aps[i0 + 2], E0F[:], W1VIEW, start=True, stop=True,
             waits=t_guard(bank_names[i0 + 2]) + [("SY", lds1)],
             inc=("MM", 1))
        prev_bcf[0] = p.val("MM")
    mm_bcf = p.val("MM")

    BCR = [TRI[:, i * 512:(i + 1) * 512] for i in range(6)]
    TMPF = TRI[:, 3072:3584]
    v_op(lambda e: tuple(e.tensor_copy(BCR[i], bank_aps[i]) for i in range(6))[-1],
         inc="VS", waits=[("MM", mm_bcf)])
    for c in range(NCH):
        for tb in range(NTB):
            slot = bsl(BUF2, c, tb)
            vv = v_op(lambda e, c=c, tb=tb, slot=slot: (
                e.tensor_mul(TMPF, bsl(BUF1, c, tb), BCR[tb * 3]),
                e.tensor_add(TMPF, TMPF, BCR[tb * 3 + 1]),
                e.tensor_mul(slot, slot, BCR[tb * 3 + 2]),
                e.tensor_add(slot, slot, TMPF))[-1],
                inc="VS")
            gdma(OUT[c][:, ts(tb, 512)], slot, pre=[("VS", p.val("VS"))])
    if DBG:
        DXN = nc.dram_tensor("DXN", [2, 128, NCH * T1], BF16, kind="ExternalOutput")
        DRW = nc.dram_tensor("DRW", [8, 1024], F32, kind="ExternalOutput")
        DS1 = nc.dram_tensor("DS1", [2, 128, 1032], F32, kind="ExternalOutput")
        DB1 = nc.dram_tensor("DB1", [128, NCH * T1], BF16, kind="ExternalOutput")
        DB2 = nc.dram_tensor("DB2", [128, NCH * T1], BF16, kind="ExternalOutput")
        DKV = nc.dram_tensor("DKV", [128, 1024], BF16, kind="ExternalOutput")
        DRK = nc.dram_tensor("DRK", [128, 200], F32, kind="ExternalOutput")
        DEX = nc.dram_tensor("DEX", [128, 3072], F32, kind="ExternalOutput")
        v_dsc = v_op(lambda e: e.tensor_copy(RKQ[:, 160:161], SCL[:]), inc="VS")
        vend = p.val("VS")
        pre = [("VS", vend), ("SY", p.val("SY"))]
        sdma(DXN[:], XNSP[:], pre=pre)
        sdma(DRW[:], RWS[:])
        sdma(DS1[:], SH1D[:])
        sdma(DB1[:], BUF1[:])
        sdma(DB2[:], BUF2[:])
        sdma(DKV[:], KVBF[:])
        sdma(DRK[:], RKQ[:])
        sdma(DEX[:], EXG[:])
    final_sy = p.val("SY")
    final_gw = p.val("GW")

    p.emit("sync", lambda e: e.wait_ge(sem["SY"], final_sy))
    p.emit("sync", lambda e: e.wait_ge(sem["GW"], final_gw))

    with nc.Block() as block:
        @block.sync
        def _(e):
            for fn in p.q["sync"]:
                fn(e)

        @block.gpsimd
        def _(e):
            for fn in p.q["gpsimd"]:
                fn(e)

        @block.vector
        def _(e):
            for fn in p.q["vector"]:
                fn(e)

        @block.scalar
        def _(e):
            for fn in p.q["scalar"]:
                fn(e)

        @block.tensor
        def _(e):
            for fn in p.q["tensor"]:
                fn(e)

    return nc



def _blk_weight(w):
    O = w.shape[0]
    return np.ascontiguousarray(
        w.reshape(O // 256, 256, NCH, 128).transpose(0, 3, 2, 1))


def prepare_in_maps(L0_attn_hidden_states, L0_attn_Wq, L0_attn_bq, L0_attn_Wk,
                    L0_attn_bk, L0_attn_Wv, L0_attn_bv, L0_attn_Wo, L0_attn_bo,
                    L0_ffn_W, L0_ffn_b, L0_ffn_A, L0_ffn_B, L0_ffn_router_W,
                    L0_ffn_router_b):
    x = np.asarray(L0_attn_hidden_states)
    Wq = np.asarray(L0_attn_Wq); bq = np.asarray(L0_attn_bq)
    Wk = np.asarray(L0_attn_Wk); bk = np.asarray(L0_attn_bk)
    Wv = np.asarray(L0_attn_Wv); bv = np.asarray(L0_attn_bv)
    Wo = np.asarray(L0_attn_Wo); bo = np.asarray(L0_attn_bo)
    Wf = np.asarray(L0_ffn_W); bf = np.asarray(L0_ffn_b)
    A = np.asarray(L0_ffn_A); B = np.asarray(L0_ffn_B)
    rW = np.asarray(L0_ffn_router_W); rb = np.asarray(L0_ffn_router_b)

    WKb = np.ascontiguousarray(
        Wk.reshape(2, 512, NCH, 128).transpose(0, 3, 2, 1))
    WVb = np.ascontiguousarray(
        Wv.reshape(2, 512, NCH, 128).transpose(0, 3, 2, 1))
    WQb = _blk_weight(Wq)
    WOb = _blk_weight(Wo)
    WFb = _blk_weight(Wf)
    ACb = np.ascontiguousarray(
        A.reshape(128, D).T.reshape(NCH, 128, 128).transpose(1, 0, 2).reshape(128, NCH * 128))
    BCb = np.ascontiguousarray(B.transpose(0, 2, 1).reshape(128, D))
    RWb = np.ascontiguousarray(
        rW.T.reshape(NCH, 128, 8).transpose(1, 0, 2).reshape(128, NCH * 8))
    BQb = np.ascontiguousarray(bq.reshape(32, 128).T)
    BOb = np.ascontiguousarray(bo.reshape(32, 128).T)
    BFb = np.ascontiguousarray(bf.reshape(32, 128).T)
    BKr = np.ascontiguousarray(bk.reshape(1, 1024))
    BVr = np.ascontiguousarray(bv.reshape(1, 1024))
    RBr = np.ascontiguousarray(rb.reshape(1, 8))
    EMk = np.zeros((128, 8), dtype=np.float32)
    for pp in range(128):
        EMk[pp, pp // 16] = 2.0
    nonce1 = int(np.random.randint(1 << 20, 1 << 30))
    nonce2 = int(np.random.randint(1 << 20, 1 << 30))

    in_maps = []
    for core in range(8):
        b = core // 4
        l = LMAP[core % 4]
        tok = (np.arange(l * 1024, l * 1024 + 2048)) % 4096
        xt = x[b, tok, :]
        Xc = np.ascontiguousarray(
            xt.reshape(2, 1024, NCH, 128).transpose(0, 3, 2, 1))
        msk = np.zeros((2, 1, T1), dtype=np.float32)
        inr = (tok >= Q_START) & (tok < Q_END_EXCL)
        msk[0, 0, :] = inr[:1024] / RCOUNT
        msk[1, 0, :] = inr[1024:] / RCOUNT
        slt = np.array([[core % 2, nonce1, nonce2, 0, 0, 0, 0, 0]], dtype=np.uint32)
        in_maps.append({
            "X": Xc, "WK": WKb, "WV": WVb, "WQ": WQb, "WO": WOb, "WFD": WFb,
            "ACBD": ACb, "BCBD": BCb, "RWTD": RWb, "BQD": BQb, "BOD": BOb,
            "BFD": BFb, "BKR": BKr, "BVR": BVr, "RBR": RBr, "MSK": msk,
            "EMKD": EMk, "SLT": slt,
        })

    return in_maps


def assemble_out(outs):
    out = np.empty((2, 4096, D), dtype=np.float32)
    for core in range(8):
        b = core // 4
        l = LMAP[core % 4]
        oc = np.asarray(outs[core])
        out[b, l * 1024:(l + 1) * 1024, :] = oc.reshape(D, T1).T
    return out


def kernel(**inputs):
    in_maps = prepare_in_maps(**inputs)
    nc = build_nc()
    res = run_bass_kernel_spmd(nc, in_maps, core_ids=list(range(8)))
    return assemble_out([res.results[c]["OUT"] for c in range(8)])


# version 17
# speedup vs baseline: 1.4308x; 1.3693x over previous
import numpy as np
import concourse.bass as bass
import concourse.mybir as mybir
from concourse.bass_utils import run_bass_kernel_spmd

F32 = mybir.dt.float32
BF16 = mybir.dt.bfloat16
U32 = mybir.dt.uint32
ACT = mybir.ActivationFunctionType
ALU = mybir.AluOpType

D = 4096
NCH = 32
T1 = 1024
NTB = 2
LMAP = [0, 2, 1, 3]
Q_START, Q_END_EXCL = 611, 4095
RCOUNT = float(Q_END_EXCL - Q_START)
EPS_LN = 1e-5
DBG = False


def ts(i, w):
    return slice(i * w, (i + 1) * w)


class Plan:
    ENGS = ("sync", "gpsimd", "vector", "scalar", "tensor")

    def __init__(self):
        self.q = {e: [] for e in self.ENGS}
        self.cnt = {}

    def bump(self, s, k):
        self.cnt[s] = self.cnt.get(s, 0) + k
        return self.cnt[s]

    def val(self, s):
        return self.cnt.get(s, 0)

    def emit(self, eng, fn):
        self.q[eng].append(fn)


def build_nc():
    nc = bass.Bass("TRN2", target_bir_lowering=False, detect_race_conditions=False)

    X = nc.dram_tensor("X", [2, 128, NCH, T1], F32, kind="ExternalInput")
    WK = nc.dram_tensor("WK", [2, 128, NCH, 512], F32, kind="ExternalInput")
    WV = nc.dram_tensor("WV", [2, 128, NCH, 512], F32, kind="ExternalInput")
    WQ = nc.dram_tensor("WQ", [16, 128, NCH, 256], F32, kind="ExternalInput")
    WO = nc.dram_tensor("WO", [16, 128, NCH, 256], F32, kind="ExternalInput")
    WFD = nc.dram_tensor("WFD", [16, 128, NCH, 256], F32, kind="ExternalInput")
    ACBD = nc.dram_tensor("ACBD", [128, NCH * 128], F32, kind="ExternalInput")
    BCBD = nc.dram_tensor("BCBD", [128, 4096], F32, kind="ExternalInput")
    RWTD = nc.dram_tensor("RWTD", [128, NCH * 8], F32, kind="ExternalInput")
    BQD = nc.dram_tensor("BQD", [128, 32], F32, kind="ExternalInput")
    BOD = nc.dram_tensor("BOD", [128, 32], F32, kind="ExternalInput")
    BFD = nc.dram_tensor("BFD", [128, 32], F32, kind="ExternalInput")
    BKR = nc.dram_tensor("BKR", [1, 1024], F32, kind="ExternalInput")
    BVR = nc.dram_tensor("BVR", [1, 1024], F32, kind="ExternalInput")
    RBR = nc.dram_tensor("RBR", [1, 8], F32, kind="ExternalInput")
    MSK = nc.dram_tensor("MSK", [2, 1, T1], F32, kind="ExternalInput")
    EMKD = nc.dram_tensor("EMKD", [128, 8], F32, kind="ExternalInput")
    SLT = nc.dram_tensor("SLT", [1, 8], U32, kind="ExternalInput")
    OUT = nc.dram_tensor("OUT", [NCH, 128, T1], F32, kind="ExternalOutput")

    XNSP = nc.dram_tensor("XNSP", [2, 128, NCH * T1], BF16, kind="Internal")
    RWS = nc.dram_tensor("RWS", [8, 1024], F32, kind="Internal")
    SH1D = nc.dram_tensor("SH1D", [2, 128, 1032], F32, kind="Internal", addr_space="Shared")
    SH1F = nc.dram_tensor("SH1F", [2, 1, 4], U32, kind="Internal", addr_space="Shared")
    SH2D = nc.dram_tensor("SH2D", [2, 128, 32], F32, kind="Internal", addr_space="Shared")
    SH2F = nc.dram_tensor("SH2F", [2, 1, 4], U32, kind="Internal", addr_space="Shared")

    BUF1 = nc.alloc_sbuf_tensor("BUF1", [128, NCH * T1], BF16)
    BUF2 = nc.alloc_sbuf_tensor("BUF2", [128, NCH * T1], BF16)
    PK = BUF2[:, 0:8192]
    PV = BUF2[:, 8192:16384]
    WR = nc.alloc_sbuf_tensor("WR", [128, 16384], BF16)
    TRI = nc.alloc_sbuf_tensor("TRI", [128, 4096], BF16)
    BCBS = nc.alloc_sbuf_tensor("BCBS", [128, 4096], BF16)
    EXG = nc.alloc_sbuf_tensor("EXG", [128, 3072], F32)
    KVBF = nc.alloc_sbuf_tensor("KVBF", [128, 1024], BF16)
    E0F = nc.alloc_sbuf_tensor("E0F", [128, 128], F32)
    E0B = nc.alloc_sbuf_tensor("E0B", [128, 128], BF16)
    ONESC = nc.alloc_sbuf_tensor("ONESC", [128, 1], BF16)
    PHIQ = nc.alloc_sbuf_tensor("PHIQ", [128, 1024], BF16)
    RKQ = nc.alloc_sbuf_tensor("RKQ", [128, 200], F32)
    QMB = nc.alloc_sbuf_tensor("QMB", [128, 32], BF16)
    BQT = nc.alloc_sbuf_tensor("BQT", [128, 96], F32)
    EMKS = nc.alloc_sbuf_tensor("EMKS", [128, 8], F32)
    SCL = nc.alloc_sbuf_tensor("SCL", [128, 1], F32)
    SLTS = nc.alloc_sbuf_tensor("SLTS", [1, 8], U32)
    FLS = nc.alloc_sbuf_tensor("FLS", [1, 4], U32)
    EPSS = nc.alloc_sbuf_tensor("EPSS", [1, 2], F32)

    M0, M1 = EXG[0:1, 0:512], EXG[0:1, 512:1024]
    RV0, RV1 = EXG[0:1, 1024:1536], EXG[0:1, 1536:2048]
    W0, W1 = EXG[0:1, 2048:2560], EXG[0:1, 2560:3072]
    MROW, RVROW = [M0, M1], [RV0, RV1]
    MVIEW = [EXG[:, 0:512], EXG[:, 512:1024]]
    RVVIEW = [EXG[:, 1024:1536], EXG[:, 1536:2048]]
    W0VIEW, W1VIEW = EXG[:, 2048:2560], EXG[:, 2560:3072]
    MT256, ET256 = EXG[:, 2048:2304], EXG[:, 2304:2560]
    MT512, ET512 = EXG[:, 2048:2560], EXG[:, 2560:3072]

    PS = [nc.alloc_psum_tensor(f"PS{i}", [128, 512], F32) for i in range(3)]
    PS3 = nc.alloc_psum_tensor("PS3", [128, 512], F32)
    PSKV = nc.alloc_psum_tensor("PSKV", [128, 1024], F32)
    PSKS = nc.alloc_psum_tensor("PSKS", [128, 512], F32)

    sems = ("GW", "SY", "WFR", "MM", "EVV", "EVS", "VS", "SV", "EX", "PX1", "PX2")
    sem = {n: nc.alloc_semaphore(n) for n in sems}

    p = Plan()
    bank_guard = {}

    def gdma(dst, src, pre=()):
        tgt = p.bump("GW", 16)

        def fn(e, dst=dst, src=src, pre=tuple(pre)):
            for s_, v_ in pre:
                e.wait_ge(sem[s_], v_)
            e.dma_start(out=dst, in_=src).then_inc(sem["GW"], 16)
        p.emit("gpsimd", fn)
        return tgt

    def sdma(dst, src, pre=()):
        tgt = p.bump("SY", 16)

        def fn(e, dst=dst, src=src, pre=tuple(pre)):
            for s_, v_ in pre:
                e.wait_ge(sem[s_], v_)
            e.dma_start(out=dst, in_=src).then_inc(sem["SY"], 16)
        p.emit("sync", fn)
        return tgt

    def ewait(eng, s, v):
        p.emit(eng, lambda e, s=s, v=v: e.wait_ge(sem[s], v))

    def einc(eng, s, k=1):
        tgt = p.bump(s, k)
        p.emit(eng, lambda e, s=s, k=k: e.nop().then_inc(sem[s], k))
        return tgt

    def t_mm(ps_ap, lhsT, rhs, start, stop, waits=(), inc=None):
        if inc is not None:
            p.bump(inc[0], inc[1])

        def fn(e, ps_ap=ps_ap, lhsT=lhsT, rhs=rhs, start=start, stop=stop,
               waits=tuple(waits), inc=inc):
            for s_, v_ in waits:
                e.wait_ge(sem[s_], v_)
            ins = e.matmul(ps_ap, lhsT, rhs, start=start, stop=stop)
            if inc is not None:
                ins.then_inc(sem[inc[0]], inc[1])
        p.emit("tensor", fn)

    def t_guard(bn):
        g = bank_guard.pop(bn, None)
        return [g] if g else []

    def v_op(fb, inc=None, waits=()):
        tgt = p.bump(inc, 1) if inc else None

        def fn(e, fb=fb, waits=tuple(waits), inc=inc):
            for s_, v_ in waits:
                e.wait_ge(sem[s_], v_)
            ins = fb(e)
            if inc:
                last = ins[-1] if isinstance(ins, (list, tuple)) else ins
                last.then_inc(sem[inc], 1)
        p.emit("vector", fn)
        return tgt

    def s_op(fb, inc=None, waits=()):
        tgt = p.bump(inc, 1) if inc else None

        def fn(e, fb=fb, waits=tuple(waits), inc=inc):
            for s_, v_ in waits:
                e.wait_ge(sem[s_], v_)
            ins = fb(e)
            if inc:
                last = ins[-1] if isinstance(ins, (list, tuple)) else ins
                last.then_inc(sem[inc], 1)
        p.emit("scalar", fn)
        return tgt

    ring = {"n": 0}

    def ring_load(src_ap, width=8192, pre=()):
        n = ring["n"]
        slot = WR[:, n % 2 * 8192: n % 2 * 8192 + width]
        waits = list(pre)
        if n >= 2:
            waits.append(("WFR", n - 1))
        tgt = gdma(slot, src_ap, pre=waits)
        ring["n"] = n + 1
        return slot, tgt

    def ring_free():
        return einc("tensor", "WFR")

    rot = {"i": 0}

    def next_bank():
        i = rot["i"] % 3
        rot["i"] += 1
        return f"PS{i}", PS[i]

    def bsl(buf, c, tb):
        return buf[:, c * T1 + tb * 512: c * T1 + (tb + 1) * 512]

    def ln_stats(src_of, sq_slots, acc, keys, pers_rows=None, src_ready=None):
        res = []
        prev_dma = None
        for tb in range(NTB):
            sum_ps, sq_ps = acc[tb]
            kb_sum, kb_sq = keys[tb]
            gsum, gsq = t_guard(kb_sum), t_guard(kb_sq)
            mm_sq = {}
            for c in range(NCH):
                src = src_of(c, tb)
                slot = sq_slots[c % 2]
                w = [("MM", mm_sq[c - 2])] if c - 2 in mm_sq else []
                if c == 0 and src_ready is not None:
                    w = w + [("VS", src_ready)]
                sq_t = s_op(lambda e, s_=slot, a=src: e.activation(
                    s_, a, ACT.Square), inc="SV", waits=w)
                sw0 = (gsum if c == 0 else [])
                if c == 0 and src_ready is not None:
                    sw0 = sw0 + [("VS", src_ready)]
                t_mm(sum_ps, ONESC[:], src, start=(c == 0), stop=(c == NCH - 1),
                     waits=sw0)
                t_mm(sq_ps, ONESC[:], slot, start=(c == 0), stop=(c == NCH - 1),
                     waits=[("SV", sq_t)] + (gsq if c == 0 else []), inc=("MM", 1))
                mm_sq[c] = p.val("MM")
            mm_done = p.val("MM")

            def fin(e, tb=tb, sum_ps=sum_ps, sq_ps=sq_ps):
                e.tensor_scalar(MROW[tb], sum_ps, 1.0 / D, None, ALU.mult)
                e.tensor_mul(W0, MROW[tb], MROW[tb])
                return e.scalar_tensor_tensor(W1, sq_ps, 1.0 / D, W0,
                                              ALU.mult, ALU.subtract)
            v_f = v_op(fin, inc="VS", waits=[("MM", mm_done), ("SV", p.val("SV"))] +
                       ([("SY", prev_dma)] if prev_dma else []))
            bank_guard[kb_sum] = ("VS", v_f)
            bank_guard[kb_sq] = ("VS", v_f)
            s_q = s_op(lambda e: e.activation(W1, W1, ACT.Sqrt, bias=EPSS[0:1, 0:1]),
                       inc="SV", waits=[("VS", v_f)])
            v_r = v_op(lambda e, tb=tb: e.reciprocal(RVROW[tb], W1),
                       inc="VS", waits=[("SV", s_q)])
            if pers_rows is not None:
                m_i, s_i = pers_rows
                sdma(RWS[m_i, ts(tb, 512)], MROW[tb], pre=[("VS", v_f)])
                prev_dma = sdma(RWS[s_i, ts(tb, 512)], W1, pre=[("VS", v_r)])
            res.append(v_r)
        return res


    v_c0 = v_op(lambda e: (
        e.memset(E0F[:], 0.0), e.memset(E0B[:], 0.0),
        e.memset(ONESC[:], 1.0), e.memset(EPSS[:], EPS_LN),
        e.memset(EXG[:], 0.0), e.memset(TRI[:], 0.0))[-1], inc="VS")
    v_consts = v_op(lambda e: (
        e.memset(E0F[0:1, :], 1.0), e.memset(E0B[0:1, :], 1.0))[-1],
        inc="VS", waits=[("VS", v_c0)])

    ld_slt = sdma(SLTS[:], SLT[:])
    sdma(BQT[:, 0:32], BQD[:])
    sdma(BQT[:, 32:64], BOD[:])
    ld_bqt = sdma(BQT[:, 64:96], BFD[:])
    ld_emk = sdma(EMKS[:], EMKD[:])

    ld_bk = gdma(TRI[0:1, 0:1024], BKR[:], pre=[("VS", v_consts)])
    ld_bv = gdma(TRI[0:1, 1024:2048], BVR[:])
    ld_bcb = gdma(BCBS[:], BCBD[:])

    kv_mm_done = [0, 0]
    spill_done = [0, 0]

    for s in (0, 1):
        pre = [] if s == 0 else [("SY", spill_done[0]), ("MM", kv_mm_done[0])]
        gdma(BUF1[:, 0:16384], X[s][:, 0:16, :], pre=pre)
        x_tgt = gdma(BUF1[:, 16384:32768], X[s][:, 16:32, :])

        ewait("tensor", "GW", x_tgt)
        ewait("vector", "GW", x_tgt)
        ewait("scalar", "GW", x_tgt)
        if s == 0:
            ewait("tensor", "VS", v_consts)
        ln_stats(lambda c, tb: bsl(BUF1, c, tb),
                 (TRI[:, 2048:2560], TRI[:, 2560:3072]),
                 acc=[(PS[0][0:1, 0:512], PS[1][0:1, 0:512])] * 2,
                 keys=[("PS0", "PS1")] * 2,
                 pers_rows=(2 * s + 0, 2 * s + 1))

        for tb in range(NTB):
            g2 = t_guard("PS2")
            t_mm(PS[2][:, 0:512], E0F[:], MVIEW[tb], start=True, stop=True,
                 waits=g2 + [("VS", p.val("VS"))])
            g0 = t_guard("PS0")
            t_mm(PS[0][:, 0:512], E0F[:], RVVIEW[tb], start=True, stop=True,
                 waits=g0, inc=("MM", 1))
            bc_done = p.val("MM")
            v_cp = v_op(lambda e: (
                e.tensor_copy(TRI[:, 3072:3584], PS[2][:, 0:512]),
                e.tensor_copy(TRI[:, 3584:4096], PS[0][:, 0:512]))[-1],
                inc="VS", waits=[("MM", bc_done)])
            bank_guard["PS2"] = ("VS", v_cp)
            bank_guard["PS0"] = ("VS", v_cp)
            for c in range(NCH):
                sl = bsl(BUF1, c, tb)
                v_op(lambda e, sl=sl: (
                    e.tensor_sub(sl, sl, TRI[:, 3072:3584]),
                    e.tensor_mul(sl, sl, TRI[:, 3584:4096]))[-1],
                    inc="VS", waits=([("VS", v_cp)] if c == 0 else []))
        v_norm = p.val("VS")

        sdma(XNSP[s][:, 0:16384], BUF1[:, 0:16384], pre=[("VS", v_norm)])
        spill_done[s] = sdma(XNSP[s][:, 16384:32768], BUF1[:, 16384:32768])

        KVW = [WR[:, 0:16384], BUF2[:, 16384:32768]]
        for wb in range(4):
            src = WK[wb] if wb < 2 else WV[wb - 2]
            n = ring["n"]
            slot = KVW[n % 2]
            wwaits = [("WFR", n - 1)] if n >= 2 else []
            wtgt = gdma(slot, src, pre=wwaits)
            ring["n"] = n + 1
            carrier = TRI[:, (wb % 2) * 512:(wb % 2) * 512 + 512] if wb < 2 else \
                TRI[:, 1024 + (wb % 2) * 512:1024 + (wb % 2) * 512 + 512]
            ewait("tensor", "GW", wtgt)
            if wb == 0:
                ewait("tensor", "GW", max(ld_bk, ld_bv))
                ewait("tensor", "VS", v_norm)
            for tile in range(8):
                bn, bap = next_bank()
                pap = bap[:, 0:512]
                g = t_guard(bn)
                for c in range(NCH):
                    t_mm(pap, BUF1[:, c * T1 + tile * 128: c * T1 + tile * 128 + 128],
                         slot[:, ts(c, 512)], start=(c == 0), stop=False,
                         waits=(g if c == 0 else []))
                t_mm(pap, E0B[:], carrier, start=False, stop=True, inc=("MM", 1))
                mm_t = p.val("MM")
                dc = tile * 1024 + (wb % 2) * 512
                extra = [("MM", kv_mm_done[0])] if (s == 1 and wb == 0 and tile == 0) else []
                if wb < 2:
                    vm = v_op(lambda e, pap=pap: e.tensor_single_scalar(
                        MT512, pap, 0.0, ALU.min), inc="VS",
                        waits=[("MM", mm_t), ("SV", p.val("SV"))] + extra)
                    se = s_op(lambda e: e.activation(ET512, MT512, ACT.Exp),
                              inc="SV", waits=[("VS", vm), ("EVV", p.val("EVV"))])
                    vp = v_op(lambda e, pap=pap, dc=dc: e.scalar_tensor_tensor(
                        PK[:, dc:dc + 512], pap, 0.0, ET512, ALU.max, ALU.add),
                        inc="EVV", waits=[("SV", se)])
                else:
                    vp = v_op(lambda e, pap=pap, dc=dc: e.tensor_copy(
                        PV[:, dc:dc + 512], pap), inc="EVV",
                        waits=[("MM", mm_t)] + extra)
                bank_guard[bn] = ("EVV", vp)
            ring_free()
        kv_evv = p.val("EVV")

        ewait("tensor", "EVV", kv_evv)
        for tile in range(8):
            for h in range(8):
                kb = PK[:, tile * 1024 + h * 128: tile * 1024 + (h + 1) * 128]
                vb = PV[:, tile * 1024 + h * 128: tile * 1024 + (h + 1) * 128]
                t_mm(PSKV[:, ts(h, 128)], kb, vb,
                     start=(s == 0 and tile == 0 and h in (0, 4)),
                     stop=(s == 1 and tile == 7 and h in (3, 7)))
                t_mm(PSKS[:, h:h + 1], kb, ONESC[:],
                     start=(s == 0 and tile == 0 and h == 0),
                     stop=(s == 1 and tile == 7 and h == 7),
                     inc=(("MM", 1) if (tile == 7 and h == 7) else None))
        kv_mm_done[s] = p.val("MM")

    exg_my = EXG[:, 0:1032]
    exg_oth = EXG[:, 1040:2072]
    v_kvst = v_op(lambda e: (
        e.tensor_copy(exg_my[:, 0:1024], PSKV[:]),
        e.tensor_copy(exg_my[:, 1024:1032], PSKS[:, 0:8]))[-1],
        inc="EVV", waits=[("MM", kv_mm_done[1])])
    bank_guard["KVA"] = ("EVV", v_kvst)
    bank_guard["KVB"] = ("EVV", v_kvst)
    bank_guard["KSA"] = ("EVV", v_kvst)

    def emit_exchange(D_, F_, my_pay, oth_pay, nonce_word, pxname, pre=()):
        def fn(e, pre=tuple(pre)):
            for s_, v_ in pre:
                e.wait_ge(sem[s_], v_)
            rmy = e.to_reg(e.value_load(SLTS[0:1, 0:1]))
            rnc = e.to_reg(e.value_load(SLTS[0:1, nonce_word:nonce_word + 1]))

            def one(my, oth):
                tag = f"{pxname}_{my}"
                e.dma_start(out=D_[my], in_=my_pay).then_inc(sem[pxname], 16)
                e.wait_ge(sem[pxname], 16)
                e.dma_start(out=F_[my], in_=SLTS[0:1, 0:4]).then_inc(sem[pxname], 16)
                with e.register("rc_" + tag) as rc, e.register("rn_" + tag) as rn:
                    e.reg_mov(rc, 1)
                    e.reg_mov(rn, 32)
                    with e.While(rc):
                        e.dma_start(out=FLS[:], in_=F_[oth]).then_inc(sem[pxname], 16)
                        e.reg_add(rn, rn, 16)
                        e.wait_ge(sem[pxname], rn)
                        rf = e.value_load(FLS[0:1, nonce_word:nonce_word + 1])
                        e.reg_alu(rc, rf, rnc, ALU.not_equal)
                    e.dma_start(out=oth_pay, in_=D_[oth]).then_inc(sem[pxname], 16)
                    e.reg_add(rn, rn, 16)
                    e.wait_ge(sem[pxname], rn)
            with e.If_eq(rmy, 0):
                one(0, 1)
            with e.Else():
                one(1, 0)
            e.nop().then_inc(sem["EX"], 1)
        p.emit("sync", fn)
        return p.bump("EX", 1)

    ex1 = emit_exchange(SH1D, SH1F, exg_my, exg_oth, 1, "PX1",
                        pre=[("EVV", v_kvst), ("SY", ld_slt)])

    def kv_combine(e):
        e.tensor_add(KVBF[:], exg_my[:, 0:1024], exg_oth[:, 0:1024])
        e.tensor_add(RKQ[:, 0:8], exg_my[:, 1024:1032], exg_oth[:, 1024:1032])
        e.tensor_scalar(RKQ[:, 0:8], RKQ[:, 0:8], 1e-10, None, ALU.add)
        ins = e.reciprocal(RKQ[:, 0:8], RKQ[:, 0:8])
        for _ in range(6):
            ins = e.tensor_copy(TRI[:, 3584:4096], BCBS[:, 0:512])
        return ins
    v_kv_ready = v_op(kv_combine, inc="VS", waits=[("EX", ex1)])

    router_done = {}
    rering_guard = {k: ("MM", kv_mm_done[1]) for k in range(4)}

    def phase_b(s, gate_vs=None, gate_evs_buf2=None):
        if s == 0:
            b1 = sdma(BUF1[:, 0:16384], XNSP[0][:, 0:16384],
                      pre=[("VS", gate_vs)])
            b2 = sdma(BUF1[:, 16384:32768], XNSP[0][:, 16384:32768])
            ewait("tensor", "SY", b2)

        phiq_free = p.val("MM")
        for om in range(16):
            slot, wtgt = ring_load(WQ[om])
            ewait("tensor", "GW", wtgt)
            for m in range(2):
                h = om * 2 + m
                kvh = h // 4
                banks = [next_bank(), next_bank()]
                gq = t_guard(banks[0][0]) + t_guard(banks[1][0])
                for c in range(NCH):
                    lhs = slot[:, c * 256 + m * 128: c * 256 + (m + 1) * 128]
                    for tb in range(NTB):
                        t_mm(banks[tb][1][:, 0:512], lhs, bsl(BUF1, c, tb),
                             start=(c == 0), stop=(c == NCH - 1),
                             waits=(gq if (c == 0 and tb == 0) else []),
                             inc=(("MM", 1) if (c == NCH - 1 and tb == 1) else None))
                mm_q = p.val("MM")
                for tb in range(NTB):
                    bn, bap = banks[tb]
                    vm = v_op(lambda e, bap=bap, h=h: e.tensor_scalar(
                        MT512, bap[:, 0:512], BQT[:, h:h + 1], 0.0, ALU.add, ALU.min),
                        inc="VS", waits=[("MM", mm_q), ("MM", phiq_free),
                                         ("SV", p.val("SV"))])
                    se = s_op(lambda e: e.activation(ET512, MT512, ACT.Exp),
                              inc="SV", waits=[("VS", vm), ("EVV", p.val("EVV"))])
                    sr = s_op(lambda e, bap=bap, h=h: e.activation(
                        bap[:, 0:512], bap[:, 0:512], ACT.Relu,
                        bias=BQT[:, h:h + 1]), inc="SV")
                    vp = v_op(lambda e, bap=bap, tb=tb: e.tensor_add(
                        PHIQ[:, ts(tb, 512)], bap[:, 0:512], ET512),
                        inc="EVV", waits=[("SV", sr)])
                    bank_guard[bn] = ("EVV", vp)
                for tb in range(NTB):
                    bn2, bap2 = next_bank()
                    g2 = t_guard(bn2)
                    t_mm(bap2[:, 0:512], KVBF[:, ts(kvh, 128)], PHIQ[:, ts(tb, 512)],
                         start=True, stop=True,
                         waits=g2 + [("EVV", p.val("EVV")), ("VS", v_kv_ready)],
                         inc=("MM", 1))
                    mm_a = p.val("MM")
                    ew = [("MM", mm_a)]
                    if gate_evs_buf2 is not None and h == 0 and tb == 0:
                        ew.append(("VS", gate_evs_buf2))
                    sa = v_op(lambda e, bap2=bap2, h=h, tb=tb, kvh=kvh: e.tensor_scalar(
                        bsl(BUF2, h, tb), bap2[:, 0:512], RKQ[:, kvh:kvh + 1], None,
                        ALU.mult),
                        inc="EVV", waits=ew)
                    bank_guard[bn2] = ("EVV", sa)
                phiq_free = p.val("MM")
            ring_free()

        ewait("tensor", "EVV", p.val("EVV"))
        ewait("tensor", "EVS", p.val("EVS"))
        sq_mm = {}
        for om in range(16):
            slot, wtgt = ring_load(WO[om])
            ewait("tensor", "GW", wtgt)
            for m in range(2):
                oc = om * 2 + m
                banks = [next_bank(), next_bank()]
                gq = t_guard(banks[0][0]) + t_guard(banks[1][0])
                for c in range(NCH):
                    lhs = slot[:, c * 256 + m * 128: c * 256 + (m + 1) * 128]
                    for tb in range(NTB):
                        t_mm(banks[tb][1][:, 0:512], lhs, bsl(BUF2, c, tb),
                             start=(c == 0), stop=(c == NCH - 1),
                             waits=(gq if (c == 0 and tb == 0) else []),
                             inc=(("MM", 1) if (c == NCH - 1 and tb == 1) else None))
                mm_o = p.val("MM")
                for tb in range(NTB):
                    bn, bap = banks[tb]
                    sqsl = TRI[:, 2048 + ((2 * oc + tb) % 2) * 512:
                               2560 + ((2 * oc + tb) % 2) * 512]
                    sw = [("MM", mm_o)]
                    if (oc - 1, tb) in sq_mm:
                        sw.append(("MM", sq_mm[(oc - 1, tb)]))
                    so = s_op(lambda e, bap=bap, oc=oc, tb=tb, sqsl=sqsl: (
                        e.activation(bsl(BUF1, oc, tb), bap[:, 0:512], ACT.Identity,
                                     bias=BQT[:, 32 + oc:33 + oc]),
                        e.activation(sqsl, bap[:, 0:512], ACT.Square,
                                     bias=BQT[:, 32 + oc:33 + oc]))[-1],
                        inc="EVS", waits=sw)
                    bank_guard[bn] = ("EVS", so)
                    sum_ps = PSKV[0:1, ts(tb, 512)]
                    sq_ps = PSKS[0:1, 0:512] if tb == 0 else PS3[0:1, 0:512]
                    kb_sum = "KVA" if tb == 0 else "KVB"
                    kb_sq = "KSA" if tb == 0 else "PS3"
                    t_mm(sum_ps, ONESC[:], bsl(BUF1, oc, tb),
                         start=(oc == 0), stop=(oc == 31),
                         waits=[("EVS", so)] + (t_guard(kb_sum) if oc == 0 else []))
                    t_mm(sq_ps, ONESC[:], sqsl, start=(oc == 0), stop=(oc == 31),
                         waits=(t_guard(kb_sq) if oc == 0 else []), inc=("MM", 1))
                    sq_mm[(oc, tb)] = p.val("MM")
            ring_free()
        mm_ost = p.val("MM")

        fin = []
        for tb in range(NTB):
            sum_ps = PSKV[0:1, ts(tb, 512)]
            sq_ps = PSKS[0:1, 0:512] if tb == 0 else PS3[0:1, 0:512]

            def fo(e, tb=tb, sum_ps=sum_ps, sq_ps=sq_ps):
                e.tensor_scalar(MROW[tb], sum_ps, 1.0 / D, None, ALU.mult)
                e.tensor_mul(W0, MROW[tb], MROW[tb])
                return e.scalar_tensor_tensor(W1, sq_ps, 1.0 / D, W0,
                                              ALU.mult, ALU.subtract)
            v_f = v_op(fo, inc="VS", waits=[("MM", mm_ost), ("EX", ex1),
                                            ("SV", p.val("SV"))])
            s_q = s_op(lambda e: e.activation(W1, W1, ACT.Sqrt, bias=EPSS[0:1, 0:1]),
                       inc="SV", waits=[("VS", v_f)])
            v_r = v_op(lambda e, tb=tb: e.reciprocal(RVROW[tb], W1),
                       inc="VS", waits=[("SV", s_q)])
            ldm = sdma(W0, RWS[2 * s + 0, ts(tb, 512)], pre=[("VS", v_r)])
            v_cc = v_op(lambda e, tb=tb: (
                e.tensor_mul(MROW[tb], MROW[tb], RVROW[tb]),
                e.scalar_tensor_tensor(MROW[tb], MROW[tb], -1.0, W0,
                                       ALU.mult, ALU.add))[-1],
                inc="VS", waits=[("SY", ldm)])
            fin.append(v_cc)
        bank_guard["KVA"] = ("VS", fin[0])
        bank_guard["KSA"] = ("VS", fin[0])
        bank_guard["KVB"] = ("VS", fin[1])
        bank_guard["PS3"] = ("VS", fin[1])

        rering = [TRI[:, i * 512:(i + 1) * 512] for i in range(4)]
        bcA, bcB, bcC = TRI[:, 2048:2560], TRI[:, 2560:3072], TRI[:, 3072:3584]
        TMPB = TRI[:, 3584:4096]
        v_x1_last = 0
        prev_bc_mm = [0]
        for tb in range(NTB):
            ldsa = sdma(W1, RWS[2 * s + 1, ts(tb, 512)],
                        pre=[("VS", fin[tb]), ("MM", prev_bc_mm[0])])
            bn, bap = next_bank()
            t_mm(bap[:, 0:512], E0F[:], RVVIEW[tb], start=True, stop=True,
                 waits=t_guard(bn) + [("VS", fin[tb])])
            bn2, bap2 = next_bank()
            t_mm(bap2[:, 0:512], E0F[:], MVIEW[tb], start=True, stop=True,
                 waits=t_guard(bn2))
            bn3, bap3 = next_bank()
            t_mm(bap3[:, 0:512], E0F[:], W1VIEW, start=True, stop=True,
                 waits=t_guard(bn3) + [("SY", ldsa)], inc=("MM", 1))
            mm_bc = p.val("MM")
            prev_bc_mm[0] = mm_bc
            v_cp = v_op(lambda e, bap=bap, bap2=bap2, bap3=bap3: (
                e.tensor_copy(bcA, bap[:, 0:512]),
                e.tensor_copy(bcB, bap2[:, 0:512]),
                e.tensor_copy(bcC, bap3[:, 0:512]))[-1],
                inc="VS", waits=[("MM", mm_bc)])
            bank_guard[bn] = ("VS", v_cp)
            bank_guard[bn2] = ("VS", v_cp)
            bank_guard[bn3] = ("VS", v_cp)
            for c in range(NCH):
                key = c % 4
                prew = [rering_guard[key]] if rering_guard[key] else []
                ldx = sdma(rering[key],
                           XNSP[s][:, c * T1 + tb * 512: c * T1 + (tb + 1) * 512],
                           pre=prew)
                vv = v_op(lambda e, c=c, tb=tb, key=key: (
                    e.tensor_mul(TMPB, bsl(BUF1, c, tb), bcA),
                    e.tensor_add(TMPB, TMPB, bcB),
                    e.tensor_mul(rering[key], rering[key], bcC),
                    e.tensor_add(bsl(BUF2, c, tb), TMPB, rering[key]))[-1],
                    inc="VS",
                    waits=[("VS", v_cp), ("SY", ldx)] if c == 0 else [("SY", ldx)])
                rering_guard[key] = ("VS", vv)
            v_x1_last = p.val("VS")

        ln_stats(lambda c, tb: bsl(BUF2, c, tb),
                 (TRI[:, 2048:2560], TRI[:, 2560:3072]),
                 acc=[(PSKV[0:1, 0:512], PSKS[0:1, 0:512]),
                      (PSKV[0:1, 512:1024], PS3[0:1, 0:512])],
                 keys=[("KVA", "KSA"), ("KVB", "PS3")],
                 pers_rows=((4, 5) if s == 0 else None),
                 src_ready=v_x1_last)

        rs_col = RKQ[:, 8:40] if s == 1 else RKQ[:, 40:72]
        rs2_col = RKQ[:, 168:200]
        for tb in range(NTB):
            ldmsk = sdma(W1[0:1, 0:512], MSK[s][:, ts(tb, 512)],
                         pre=[("VS", p.val("VS"))])
            bn, bap = next_bank()
            t_mm(bap[:, 0:512], E0F[:], MVIEW[tb], start=True, stop=True,
                 waits=t_guard(bn) + [("VS", p.val("VS"))])
            bn2, bap2 = next_bank()
            t_mm(bap2[:, 0:512], E0F[:], RVVIEW[tb], start=True, stop=True,
                 waits=t_guard(bn2))
            bn3, bap3 = next_bank()
            t_mm(bap3[:, 0:512], E0F[:], W1VIEW, start=True, stop=True,
                 waits=t_guard(bn3) + [("SY", ldmsk)], inc=("MM", 1))
            mm_bc = p.val("MM")
            v_cp = v_op(lambda e, bap=bap, bap2=bap2, bap3=bap3: (
                e.tensor_copy(bcA, bap[:, 0:512]),
                e.tensor_copy(bcB, bap2[:, 0:512]),
                e.tensor_copy(bcC, bap3[:, 0:512]))[-1],
                inc="VS", waits=[("MM", mm_bc)])
            bank_guard[bn] = ("VS", v_cp)
            bank_guard[bn2] = ("VS", v_cp)
            bank_guard[bn3] = ("VS", v_cp)
            for c in range(NCH):
                sl = bsl(BUF2, c, tb)
                v_op(lambda e, sl=sl: (
                    e.tensor_sub(sl, sl, bcA),
                    e.tensor_mul(sl, sl, bcB))[-1],
                    inc="VS", waits=([("VS", v_cp)] if c == 0 else []))
                v_op(lambda e, sl=sl: e.tensor_mul(TMPB, sl, bcC), inc="VS")
                dst = rs_col[:, c:c + 1] if tb == 0 else rs2_col[:, c:c + 1]
                v_op(lambda e, dst=dst: e.reduce_sum(
                    dst, TMPB, mybir.AxisListType.X), inc="VS")
        v_op(lambda e, rs_col=rs_col: e.tensor_add(rs_col, rs_col, rs2_col),
             inc="VS")
        router_done[s] = p.val("VS")
        return v_x1_last

    v_x1_s1 = phase_b(1)
    v_x1_s0 = phase_b(0, gate_vs=v_x1_s1, gate_evs_buf2=router_done[1])

    v_pay = v_op(lambda e: e.tensor_add(RKQ[:, 72:104], RKQ[:, 8:40], RKQ[:, 40:72]),
                 inc="VS", waits=[("VS", router_done[0])])
    ex2 = emit_exchange(SH2D, SH2F, RKQ[:, 72:104], RKQ[:, 104:136], 2, "PX2",
                        pre=[("EX", ex1), ("VS", v_pay)])
    def qm_fold(e):
        ins = e.tensor_add(RKQ[:, 8:40], RKQ[:, 72:104], RKQ[:, 104:136])
        for _ in range(6):
            ins = e.tensor_copy(TRI[:, 3584:4096], BCBS[:, 0:512])
        return ins
    v_qm = v_op(qm_fold, inc="VS", waits=[("EX", ex2)])

    rw_slot, rw_tgt = ring_load(RWTD[:], width=256)
    PROD = TRI[:, 0:256]
    v_prod = None
    for c in range(NCH):
        w = [("VS", v_qm), ("GW", rw_tgt)] if c == 0 else []
        v_prod = v_op(lambda e, c=c: e.tensor_scalar(
            PROD[:, ts(c, 8)], rw_slot[:, ts(c, 8)], RKQ[:, 8 + c:9 + c], None,
            ALU.mult), inc="VS", waits=w)
    g = t_guard("PS0")
    for c in range(NCH):
        t_mm(PS[0][0:1, 0:8], ONESC[:], PROD[:, ts(c, 8)],
             start=(c == 0), stop=(c == NCH - 1),
             waits=(g + [("VS", v_prod)] if c == 0 else []),
             inc=(("MM", 1) if c == NCH - 1 else None))
    mm_lg = p.val("MM")
    ring_free()
    ld_rb = sdma(W1[0:1, 0:8], RBR[:], pre=[])

    v_sm1 = v_op(lambda e: e.tensor_add(W0[0:1, 0:8], PS[0][0:1, 0:8],
                                        W1[0:1, 0:8]),
                 inc="VS", waits=[("MM", mm_lg), ("SY", ld_rb)])
    bank_guard["PS0"] = ("VS", v_sm1)
    if DBG:
        sdma(RWS[6, 0:256], EXG[0:1, 2304:2560], pre=[("VS", v_sm1)])
        sdma(RWS[7, 0:8], W0[0:1, 0:8])
    g = t_guard("PS1")
    t_mm(PS[1][:, 0:8], E0F[:], W0VIEW[:, 0:8], start=True, stop=True,
         waits=g + [("VS", v_sm1)], inc=("MM", 1))
    mm_rbc = p.val("MM")
    s_exp = s_op(lambda e: e.activation(RKQ[:, 144:152], PS[1][:, 0:8], ACT.Exp),
                 inc="SV", waits=[("MM", mm_rbc)])
    def scl_fold(e):
        def age(n=4):
            for _ in range(n):
                e.tensor_copy(TRI[:, 3584:4096], BCBS[:, 0:512])
        e.reduce_sum(RKQ[:, 152:153], RKQ[:, 144:152], mybir.AxisListType.X)
        age()
        e.reciprocal(RKQ[:, 153:154], RKQ[:, 152:153])
        e.tensor_mul(RKQ[:, 136:144], RKQ[:, 144:152], EMKS[:])
        age()
        e.reduce_sum(RKQ[:, 154:155], RKQ[:, 136:144], mybir.AxisListType.X)
        age()
        ins = e.tensor_mul(SCL[:], RKQ[:, 154:155], RKQ[:, 153:154])
        age(6)
        return e.tensor_copy(TRI[:, 3584:4096], BCBS[:, 0:512])
    v_scl = v_op(scl_fold, inc="VS", waits=[("SV", s_exp), ("SY", ld_emk)])
    bank_guard["PS1"] = ("VS", v_scl)

    ac_slot, ac_tgt = ring_load(ACBD[:], width=4096)
    ewait("tensor", "GW", ac_tgt)
    ewait("tensor", "VS", router_done[0])
    tsc = [TRI[:, 0:512], TRI[:, 512:1024]]
    for tb in range(NTB):
        bn, bap = next_bank()
        g = t_guard(bn)
        for c in range(NCH):
            t_mm(bap[:, 0:512], ac_slot[:, ts(c, 128)], bsl(BUF2, c, tb),
                 start=(c == 0), stop=(c == NCH - 1),
                 waits=(g if c == 0 else []), inc=(("MM", 1) if c == NCH - 1 else None))
        mm_t = p.val("MM")
        vt = v_op(lambda e, bap=bap, tb=tb: e.tensor_scalar(
            tsc[tb], bap[:, 0:512], SCL[:, 0:1], None, ALU.mult),
            inc="EVV", waits=[("MM", mm_t), ("VS", v_scl)])
        bank_guard[bn] = ("EVV", vt)
    ring_free()
    evv_t = p.val("EVV")

    ewait("tensor", "GW", ld_bcb)
    sq_mm = {}
    for om in range(16):
        slot, wtgt = ring_load(WFD[om])
        ewait("tensor", "GW", wtgt)
        for m in range(2):
            oc = om * 2 + m
            banks = [next_bank(), next_bank()]
            gq = t_guard(banks[0][0]) + t_guard(banks[1][0])
            for c in range(NCH):
                lhs = slot[:, c * 256 + m * 128: c * 256 + (m + 1) * 128]
                for tb in range(NTB):
                    t_mm(banks[tb][1][:, 0:512], lhs, bsl(BUF2, c, tb),
                         start=(c == 0), stop=False,
                         waits=(gq if (c == 0 and tb == 0) else []))
            for tb in range(NTB):
                t_mm(banks[tb][1][:, 0:512], BCBS[:, ts(oc, 128)], tsc[tb],
                     start=False, stop=True,
                     waits=([("EVV", evv_t)] if (oc == 0 and tb == 0) else []),
                     inc=(("MM", 1) if tb == 1 else None))
            mm_o = p.val("MM")
            for tb in range(NTB):
                bn, bap = banks[tb]
                sqsl = TRI[:, 2048 + ((2 * oc + tb) % 2) * 512:
                           2560 + ((2 * oc + tb) % 2) * 512]
                sw = [("MM", mm_o)]
                if (oc - 1, tb) in sq_mm:
                    sw.append(("MM", sq_mm[(oc - 1, tb)]))
                so = s_op(lambda e, bap=bap, oc=oc, tb=tb, sqsl=sqsl: (
                    e.activation(bsl(BUF1, oc, tb), bap[:, 0:512], ACT.Identity,
                                 bias=BQT[:, 64 + oc:65 + oc]),
                    e.activation(sqsl, bap[:, 0:512], ACT.Square,
                                 bias=BQT[:, 64 + oc:65 + oc]))[-1],
                    inc="EVS", waits=sw)
                bank_guard[bn] = ("EVS", so)
                sum_ps = PSKV[0:1, ts(tb, 512)]
                sq_ps = PSKS[0:1, 0:512] if tb == 0 else PS3[0:1, 0:512]
                kb_sum = "KVA" if tb == 0 else "KVB"
                kb_sq = "KSA" if tb == 0 else "PS3"
                t_mm(sum_ps, ONESC[:], bsl(BUF1, oc, tb),
                     start=(oc == 0), stop=(oc == 31),
                     waits=[("EVS", so)] + (t_guard(kb_sum) if oc == 0 else []))
                t_mm(sq_ps, ONESC[:], sqsl, start=(oc == 0), stop=(oc == 31),
                     waits=(t_guard(kb_sq) if oc == 0 else []), inc=("MM", 1))
                sq_mm[(oc, tb)] = p.val("MM")
        ring_free()
    mm_o2 = p.val("MM")

    fin2 = []
    for tb in range(NTB):
        sum_ps = PSKV[0:1, ts(tb, 512)]
        sq_ps = PSKS[0:1, 0:512] if tb == 0 else PS3[0:1, 0:512]

        def fo2(e, tb=tb, sum_ps=sum_ps, sq_ps=sq_ps):
            e.tensor_scalar(MROW[tb], sum_ps, 1.0 / D, None, ALU.mult)
            e.tensor_mul(W0, MROW[tb], MROW[tb])
            return e.scalar_tensor_tensor(W1, sq_ps, 1.0 / D, W0, ALU.mult, ALU.subtract)
        v_f = v_op(fo2, inc="VS", waits=[("MM", mm_o2), ("SV", p.val("SV"))])
        s_q = s_op(lambda e: e.activation(W1, W1, ACT.Sqrt, bias=EPSS[0:1, 0:1]),
                   inc="SV", waits=[("VS", v_f)])
        v_r = v_op(lambda e, tb=tb: e.reciprocal(RVROW[tb], W1), inc="VS",
                   waits=[("SV", s_q)])
        ldm1 = sdma(W0, RWS[4, ts(tb, 512)], pre=[("VS", v_r)])
        v_cc = v_op(lambda e, tb=tb: (
            e.tensor_mul(MROW[tb], MROW[tb], RVROW[tb]),
            e.scalar_tensor_tensor(MROW[tb], MROW[tb], -1.0, W0, ALU.mult, ALU.add))[-1],
            inc="VS", waits=[("SY", ldm1)])
        fin2.append(v_cc)
    bank_guard["KVA"] = ("VS", fin2[0])
    bank_guard["KSA"] = ("VS", fin2[0])
    bank_guard["KVB"] = ("VS", fin2[1])
    bank_guard["PS3"] = ("VS", fin2[1])

    bc_banks = {}
    bank_names = ["PS0", "PS1", "PS2", "KVA", "KVB", "KSA"]
    bank_aps = [PS[0][:, 0:512], PS[1][:, 0:512], PS[2][:, 0:512],
                PSKV[:, 0:512], PSKV[:, 512:1024], PSKS[:, 0:512]]
    ldx1 = [0, 0]
    prev_bcf = [0]
    for tb in range(NTB):
        lds1 = sdma(W1, RWS[5, ts(tb, 512)],
                    pre=[("VS", fin2[tb]), ("MM", prev_bcf[0])])
        ldx1[tb] = lds1
        i0 = tb * 3
        t_mm(bank_aps[i0], E0F[:], RVVIEW[tb], start=True, stop=True,
             waits=t_guard(bank_names[i0]) + [("VS", fin2[tb])])
        t_mm(bank_aps[i0 + 1], E0F[:], MVIEW[tb], start=True, stop=True,
             waits=t_guard(bank_names[i0 + 1]))
        t_mm(bank_aps[i0 + 2], E0F[:], W1VIEW, start=True, stop=True,
             waits=t_guard(bank_names[i0 + 2]) + [("SY", lds1)],
             inc=("MM", 1))
        prev_bcf[0] = p.val("MM")
    mm_bcf = p.val("MM")

    BCR = [TRI[:, i * 512:(i + 1) * 512] for i in range(6)]
    TMPF = TRI[:, 3072:3584]
    v_op(lambda e: tuple(e.tensor_copy(BCR[i], bank_aps[i]) for i in range(6))[-1],
         inc="VS", waits=[("MM", mm_bcf)])
    for c in range(NCH):
        for tb in range(NTB):
            slot = bsl(BUF2, c, tb)
            vv = v_op(lambda e, c=c, tb=tb, slot=slot: (
                e.tensor_mul(TMPF, bsl(BUF1, c, tb), BCR[tb * 3]),
                e.tensor_add(TMPF, TMPF, BCR[tb * 3 + 1]),
                e.tensor_mul(slot, slot, BCR[tb * 3 + 2]),
                e.tensor_add(slot, slot, TMPF))[-1],
                inc="VS")
            gdma(OUT[c][:, ts(tb, 512)], slot, pre=[("VS", p.val("VS"))])
    if DBG:
        DXN = nc.dram_tensor("DXN", [2, 128, NCH * T1], BF16, kind="ExternalOutput")
        DRW = nc.dram_tensor("DRW", [8, 1024], F32, kind="ExternalOutput")
        DS1 = nc.dram_tensor("DS1", [2, 128, 1032], F32, kind="ExternalOutput")
        DB1 = nc.dram_tensor("DB1", [128, NCH * T1], BF16, kind="ExternalOutput")
        DB2 = nc.dram_tensor("DB2", [128, NCH * T1], BF16, kind="ExternalOutput")
        DKV = nc.dram_tensor("DKV", [128, 1024], BF16, kind="ExternalOutput")
        DRK = nc.dram_tensor("DRK", [128, 200], F32, kind="ExternalOutput")
        DEX = nc.dram_tensor("DEX", [128, 3072], F32, kind="ExternalOutput")
        v_dsc = v_op(lambda e: e.tensor_copy(RKQ[:, 160:161], SCL[:]), inc="VS")
        vend = p.val("VS")
        pre = [("VS", vend), ("SY", p.val("SY"))]
        sdma(DXN[:], XNSP[:], pre=pre)
        sdma(DRW[:], RWS[:])
        sdma(DS1[:], SH1D[:])
        sdma(DB1[:], BUF1[:])
        sdma(DB2[:], BUF2[:])
        sdma(DKV[:], KVBF[:])
        sdma(DRK[:], RKQ[:])
        sdma(DEX[:], EXG[:])
    final_sy = p.val("SY")
    final_gw = p.val("GW")

    p.emit("sync", lambda e: e.wait_ge(sem["SY"], final_sy))
    p.emit("sync", lambda e: e.wait_ge(sem["GW"], final_gw))

    with nc.Block() as block:
        @block.sync
        def _(e):
            for fn in p.q["sync"]:
                fn(e)

        @block.gpsimd
        def _(e):
            for fn in p.q["gpsimd"]:
                fn(e)

        @block.vector
        def _(e):
            for fn in p.q["vector"]:
                fn(e)

        @block.scalar
        def _(e):
            for fn in p.q["scalar"]:
                fn(e)

        @block.tensor
        def _(e):
            for fn in p.q["tensor"]:
                fn(e)

    return nc



def _blk_weight(w):
    O = w.shape[0]
    return np.ascontiguousarray(
        w.reshape(O // 256, 256, NCH, 128).transpose(0, 3, 2, 1))


def prepare_in_maps(L0_attn_hidden_states, L0_attn_Wq, L0_attn_bq, L0_attn_Wk,
                    L0_attn_bk, L0_attn_Wv, L0_attn_bv, L0_attn_Wo, L0_attn_bo,
                    L0_ffn_W, L0_ffn_b, L0_ffn_A, L0_ffn_B, L0_ffn_router_W,
                    L0_ffn_router_b):
    x = np.asarray(L0_attn_hidden_states)
    Wq = np.asarray(L0_attn_Wq); bq = np.asarray(L0_attn_bq)
    Wk = np.asarray(L0_attn_Wk); bk = np.asarray(L0_attn_bk)
    Wv = np.asarray(L0_attn_Wv); bv = np.asarray(L0_attn_bv)
    Wo = np.asarray(L0_attn_Wo); bo = np.asarray(L0_attn_bo)
    Wf = np.asarray(L0_ffn_W); bf = np.asarray(L0_ffn_b)
    A = np.asarray(L0_ffn_A); B = np.asarray(L0_ffn_B)
    rW = np.asarray(L0_ffn_router_W); rb = np.asarray(L0_ffn_router_b)

    WKb = np.ascontiguousarray(
        Wk.reshape(2, 512, NCH, 128).transpose(0, 3, 2, 1))
    WVb = np.ascontiguousarray(
        Wv.reshape(2, 512, NCH, 128).transpose(0, 3, 2, 1))
    WQb = _blk_weight(Wq)
    WOb = _blk_weight(Wo)
    WFb = _blk_weight(Wf)
    ACb = np.ascontiguousarray(
        A.reshape(128, D).T.reshape(NCH, 128, 128).transpose(1, 0, 2).reshape(128, NCH * 128))
    BCb = np.ascontiguousarray(B.transpose(0, 2, 1).reshape(128, D))
    RWb = np.ascontiguousarray(
        rW.T.reshape(NCH, 128, 8).transpose(1, 0, 2).reshape(128, NCH * 8))
    BQb = np.ascontiguousarray(bq.reshape(32, 128).T)
    BOb = np.ascontiguousarray(bo.reshape(32, 128).T)
    BFb = np.ascontiguousarray(bf.reshape(32, 128).T)
    BKr = np.ascontiguousarray(bk.reshape(1, 1024))
    BVr = np.ascontiguousarray(bv.reshape(1, 1024))
    RBr = np.ascontiguousarray(rb.reshape(1, 8))
    EMk = np.zeros((128, 8), dtype=np.float32)
    for pp in range(128):
        EMk[pp, pp // 16] = 2.0
    nonce1 = int(np.random.randint(1 << 20, 1 << 30))
    nonce2 = int(np.random.randint(1 << 20, 1 << 30))

    in_maps = []
    for core in range(8):
        b = core // 4
        l = LMAP[core % 4]
        tok = (np.arange(l * 1024, l * 1024 + 2048)) % 4096
        xt = x[b, tok, :]
        Xc = np.ascontiguousarray(
            xt.reshape(2, 1024, NCH, 128).transpose(0, 3, 2, 1))
        msk = np.zeros((2, 1, T1), dtype=np.float32)
        inr = (tok >= Q_START) & (tok < Q_END_EXCL)
        msk[0, 0, :] = inr[:1024] / RCOUNT
        msk[1, 0, :] = inr[1024:] / RCOUNT
        slt = np.array([[core % 2, nonce1, nonce2, 0, 0, 0, 0, 0]], dtype=np.uint32)
        in_maps.append({
            "X": Xc, "WK": WKb, "WV": WVb, "WQ": WQb, "WO": WOb, "WFD": WFb,
            "ACBD": ACb, "BCBD": BCb, "RWTD": RWb, "BQD": BQb, "BOD": BOb,
            "BFD": BFb, "BKR": BKr, "BVR": BVr, "RBR": RBr, "MSK": msk,
            "EMKD": EMk, "SLT": slt,
        })

    return in_maps


def assemble_out(outs):
    out = np.empty((2, 4096, D), dtype=np.float32)
    for core in range(8):
        b = core // 4
        l = LMAP[core % 4]
        oc = np.asarray(outs[core])
        out[b, l * 1024:(l + 1) * 1024, :] = oc.reshape(D, T1).T
    return out


def kernel(**inputs):
    in_maps = prepare_in_maps(**inputs)
    nc = build_nc()
    res = run_bass_kernel_spmd(nc, in_maps, core_ids=list(range(8)))
    return assemble_out([res.results[c]["OUT"] for c in range(8)])
